# version 01
# baseline (speedup 1.0000x reference)
import numpy as np
import ml_dtypes
from contextlib import ExitStack

import concourse.bass as bass
import concourse.mybir as mybir
from concourse.bass_utils import run_bass_kernel_spmd

F32 = mybir.dt.float32
BF16 = mybir.dt.bfloat16
I32 = mybir.dt.int32
AF = mybir.ActivationFunctionType
ALU = mybir.AluOpType
AX = mybir.AxisListType
BFD = ml_dtypes.bfloat16

B, S, D = 4, 2048, 2048
NST, RK, KC = 16, 128, 4
HID = 4 * D
HH = HID // 2
H1, N1 = 32, 64
TL = S // 2
EPS_LN = 1e-5
EPS_GN = 1e-5 * 64.0
EPS_RMS = 1e-5

DEBUG = 0
FFN_PRE_ONLY = 0
PHASES = 1
PAIRS = [[0, 1], [2, 3], [4, 5], [6, 7]]


class Em:
    ENGS = ("sync", "vector", "scalar", "tensor", "gpsimd")

    def __init__(self):
        self.recs = {e: [] for e in self.ENGS}
        self.marked = set()

    def op(self, eng, fn, after=(), dma=False):
        for tok in after:
            if tok is None:
                continue
            peng, pidx = tok
            self.marked.add((peng, pidx))
            self.recs[eng].append(("wait", (peng, pidx)))
        idx = len(self.recs[eng])
        self.recs[eng].append(("do", fn, dma))
        return (eng, idx)

    def finalize(self, sems):
        val_of = {}
        for eng in self.ENGS:
            cnt = {True: 0, False: 0}
            for i, rec in enumerate(self.recs[eng]):
                if rec[0] == "do" and (rec[2] or (eng, i) in self.marked):
                    self.marked.add((eng, i))
                    isd = bool(rec[2])
                    cnt[isd] += 16 if isd else 1
                    val_of[(eng, i)] = ((eng, isd), cnt[isd])

        def make(eng):
            recs = self.recs[eng]
            marked = self.marked

            def run(e):
                waited = {}
                for i, rec in enumerate(recs):
                    if rec[0] == "wait":
                        key, v = val_of[rec[1]]
                        if waited.get(key, 0) >= v:
                            continue
                        waited[key] = v
                        e.wait_ge(sems[key], v)
                    else:
                        inst = rec[1](e)
                        if (eng, i) in marked:
                            isd = bool(rec[2])
                            inst.then_inc(sems[(eng, isd)], 16 if isd else 1)
            return run
        return make


class Rgn:
    def __init__(self):
        self.w = None
        self.readers = []

    def bw(self):
        d = list(self.readers)
        if self.w is not None:
            d.append(self.w)
        return d

    def wrote(self, tok):
        self.w = tok
        self.readers = []
        return tok

    def br(self):
        return [self.w] if self.w is not None else []

    def read(self, tok):
        self.readers.append(tok)
        return tok


def build_program():
    nc = bass.Bass("TRN2", target_bir_lowering=False, detect_race_conditions=False)
    E = Em()

    def din(name, shape, dtype=F32):
        return nc.dram_tensor(name, list(shape), dtype, kind="ExternalInput").ap()

    xb = din("xb", (S, D))
    xsl = din("xsl", (TL, D))
    w_inx = din("w_inx", (D, D))
    w_ing = din("w_ing", (D, D))
    convw = din("convw", (D, KC))
    convb = din("convb", (128, 16))
    w_xp = din("w_xp", (D, RK + 2 * NST))
    w_dt = din("w_dt", (RK, D))
    dt_bias = din("dt_bias", (128, 16))
    a_log = din("a_log", (D, NST))
    d_vec = din("d_vec", (128, 16))
    w_op = din("w_op", (D, D))
    w_gate = din("w_gate", (D, HH))
    w_val = din("w_val", (D, HH))
    w_out0 = din("w_out0", (HH, D))
    b_gate = din("b_gate", (1, HH))
    b_val = din("b_val", (1, HH))
    b_out0 = din("b_out0", (1, D))
    maa_c = din("maa_c", (128, 128))
    w_maa1 = din("w_maa1", (D, 5 * 32))
    w_maa2 = din("w_maa2", (5 * 32, D))
    w_dec1 = din("w_dec1", (D, 64))
    w_dec2 = din("w_dec2", (64, D))
    u_faa = din("u_faa", (128, 16))
    t_state = din("t_state", (D, N1))
    w_r = din("w_r", (D, D))
    w_k = din("w_k", (D, D))
    w_v = din("w_v", (D, D))
    w_g = din("w_g", (D, D))
    w_o = din("w_o", (D, D))
    lnw_row = din("lnw_row", (1, D))
    lnb_row = din("lnb_row", (1, D))
    w_gv_v = din("w_gv_v", (D, HH))
    w_gv_g = din("w_gv_g", (D, HH))
    w_out1 = din("w_out1", (HH, D))
    nw_row = din("nw_row", (1, D))
    cflag = din("cflag", (1, 8))
    identb = din("identb", (128, 128), BF16)
    maskst = din("maskst", (128, 128), I32)
    identf = din("identf", (128, 128))

    out = nc.dram_tensor("out", [TL, D], F32, kind="ExternalOutput").ap()
    dbg = {}
    for bit, nm in ((1, "dbg_attn0"), (2, "dbg_res1"), (4, "dbg_res2")):
        if DEBUG & bit:
            dbg[bit] = nc.dram_tensor(nm, [TL, D], F32, kind="ExternalOutput").ap()
    for bit, nm in ((8, "dbg_xlnT"), (16, "dbg_xc"), (64, "dbg_yh")):
        if DEBUG & bit:
            dbg[bit] = nc.dram_tensor(nm, [128, 32768], BF16, kind="ExternalOutput").ap()
    if DEBUG & 32:
        dbg[32] = nc.dram_tensor("dbg_dti", [128, 2048], BF16, kind="ExternalOutput").ap()
        dbg[33] = nc.dram_tensor("dbg_bc", [32, 2048], BF16, kind="ExternalOutput").ap()
    if DEBUG & 128:
        dbg[128] = nc.dram_tensor("dbg_sm", [128, 768], F32, kind="ExternalOutput").ap()
        dbg[129] = nc.dram_tensor("dbg_gsrc", [1, 8], F32, kind="ExternalOutput").ap()
        dbg[130] = nc.dram_tensor("dbg_gdst", [2, 8], F32, kind="ExternalOutput").ap()
        dbg[131] = nc.dram_tensor("dbg_gh", [1, 8], F32, kind="ExternalOutput").ap()
        dbg[132] = nc.dram_tensor("dbg_gmid", [1, 24], F32, kind="ExternalOutput").ap()
        dbg[133] = nc.dram_tensor("dbg_g3", [128, 32], F32, kind="ExternalOutput").ap()
    if DEBUG & 256:
        dbg[256] = nc.dram_tensor("dbg_bu", [128, 32768], BF16, kind="ExternalOutput").ap()
    if DEBUG & 512:
        dbg[512] = nc.dram_tensor("dbg_a2", [128, 32768], BF16, kind="ExternalOutput").ap()
    if DEBUG & 1024:
        dbg[1024] = nc.dram_tensor("dbg_y1", [TL, D], BF16, kind="ExternalOutput").ap()
        dbg[1025] = nc.dram_tensor("dbg_y2", [TL, D], BF16, kind="ExternalOutput").ap()
        dbg[1026] = nc.dram_tensor("dbg_sin", [128, 1024], mybir.dt.float32, kind="ExternalOutput").ap()
        dbg[1027] = nc.dram_tensor("dbg_ew", [128, 16384], BF16, kind="ExternalOutput").ap()

    def hbm(name, shape, dtype=F32):
        return nc.dram_tensor(name, list(shape), dtype, kind="Internal")

    gate_scr = hbm("gate_scr", (128, 16 * S), BF16)
    bc_scr = hbm("bc_scr", (32, S), BF16)
    rs0_src = hbm("rs0_src", (S, D), BF16)
    rs0_dst = hbm("rs0_dst", (TL, D), BF16)
    res_hbm = hbm("res_hbm", (TL, D))
    ln2_hbm = hbm("ln2_hbm", (TL, D))
    ln1_hbm = hbm("ln1_hbm", (TL + 1, D), BF16)
    ag_gam_src = hbm("ag_gam_src", (1, 8))
    ag_gam_dst = hbm("ag_gam_dst", (2, 8))
    bar_src = hbm("bar_src", (1, 8))
    bar_dst = hbm("bar_dst", (2, 8))
    gam_hbm = hbm("gam_hbm", (1, 8))
    scrow_hbm = hbm("scrow_hbm", (1, 4 * S), BF16)
    tern = {k: hbm("tern_" + k, (D, HH) if k in ("g0", "v0", "g1", "v1") else (HH, D), BF16)
            for k in ("g0", "v0", "o0", "g1", "v1", "o1")}
    agx_src = [hbm(f"agx_src{l}", (128, 16 * TL), BF16) for l in range(2)]
    agx_dst = [hbm(f"agx_dst{l}", (2, 128, 16 * TL), BF16) for l in range(2)]
    agsc_src = [hbm(f"agsc_src{l}", (1, TL)) for l in range(2)]
    agsc_dst = [hbm(f"agsc_dst{l}", (2, TL)) for l in range(2)]
    armx_src = [[hbm(f"armx_src{l}_{hf}", (1, TL)) for hf in range(2)] for l in range(2)]
    armx_dst = [[hbm(f"armx_dst{l}_{hf}", (2, TL)) for hf in range(2)] for l in range(2)]
    rsf_src = [hbm(f"rsf_src{l}", (S, D), BF16) for l in range(2)]
    rsf_dst = [hbm(f"rsf_dst{l}", (TL, D), BF16) for l in range(2)]
    agtok_src = hbm("agtok_src", (1, D), BF16)
    agtok_dst = hbm("agtok_dst", (2, D), BF16)
    agst_src = hbm("agst_src", (128, 1040))
    agst_dst = hbm("agst_dst", (2, 128, 1040))
    v_hbm = hbm("v_hbm", (TL, D), BF16)
    tnh_hbm = hbm("tnh_hbm", (160, TL), BF16)
    y_scr = hbm("y_scr", (TL, D), BF16)
    y2_scr = hbm("y2_scr", (TL, D), BF16)
    x1T_hbm = hbm("x1T_hbm", (D, TL), BF16)
    xxT_hbm = hbm("xxT_hbm", (D, TL), BF16)

    with ExitStack() as ctx:
        sbt = lambda name, shape, dtype=F32: ctx.enter_context(nc.sbuf_tensor(name, list(shape), dtype))
        pst = lambda name, shape, dtype=F32: ctx.enter_context(nc.psum_tensor(name, list(shape), dtype))

        A = sbt("arenaA", (128, 32768), BF16)
        Bu = sbt("arenaB", (128, 32768), BF16)
        C = sbt("arenaC", (128, 6144))
        Dp = sbt("arenaD", (128, 4096), BF16)
        WF = sbt("wfbuf", (128, 2048))
        WB = sbt("wbuf", (128, 2048), BF16)
        WDT = sbt("wdtbuf", (128, 2048), BF16)
        QI = sbt("qibuf", (128, 2048), I32)
        GB = sbt("gbuf", (128, 2048))
        SM = sbt("smalls", (128, 768))
        SMB = sbt("smallsb", (128, 512), BF16)
        T32 = sbt("t32", (32, 1024), BF16)
        BCb = sbt("bcb", (32, 2048), BF16)
        IDB = sbt("identb_s", (128, 128), BF16)
        MSK = sbt("mask_s", (128, 128), I32)
        IDF = sbt("identf_s", (128, 128))
        ZRO = sbt("zro", (128, 128))
        ROWB = sbt("rowb", (1, 2048), BF16)

        PM = [pst(f"pm{i}", (128, 512)) for i in range(4)]
        PT = pst("ptr", (128, 512), BF16)
        PSC = pst("psc", (128, 512))
        PSY = pst("psy", (128, 512))

        WFb = WF[:].bitcast(BF16)
        QIb = QI[:].bitcast(BF16)
        T64 = GB[:].bitcast(BF16)

        sems = {}
        for n in Em.ENGS:
            sems[(n, False)] = ctx.enter_context(nc.semaphore("sx_" + n))
            sems[(n, True)] = ctx.enter_context(nc.semaphore("sd_" + n))
        block = ctx.enter_context(nc.Block())

        def dma(out_ap, in_ap, after=(), eng="sync", slow=False):
            return E.op(eng, lambda e, o=out_ap, i=in_ap, sl=slow: e.dma_start(
                out=o, in_=i, allow_slow_non_contiguous=sl), after=after, dma=True)

        def vec(fn, after=()):
            return E.op("vector", fn, after=after)

        def sca(fn, after=()):
            return E.op("scalar", fn, after=after)

        def ten(fn, after=()):
            return E.op("tensor", fn, after=after)

        def coll(kind, op_, in_ap, out_ap, groups, after=()):
            return E.op("gpsimd",
                        lambda e, k=kind, o=op_, i=in_ap, ou=out_ap, g=groups:
                        e.collective_compute(k, o, replica_groups=g, ins=[i], outs=[ou]),
                        after=after)

        def mm(psum_ap, lhsT, rhs, start, stop, after=()):
            return ten(lambda e, p=psum_ap, l=lhsT, r=rhs, s1=start, s2=stop:
                       e.matmul(p, l, r, start=s1, stop=s2), after=after)

        R = {name: Rgn() for name in
             ("A", "Bu", "C0", "C1", "C2", "Dp0", "Dp1", "WF", "WB", "WDT", "QI",
              "T32", "BCb", "ROWF", "ROWB", "SMB0", "SMB1",
              "PM0", "PM1", "PM2", "PM3", "PT", "PSC", "PSY",
              "resH", "SMsc", "SMsc2", "SMe")}
        PMr = [R["PM0"], R["PM1"], R["PM2"], R["PM3"]]

        SM_A = 0
        SM_CONVW = 256
        SM_CONVB = 320
        SM_DTB = 336
        SM_DV = 352
        SM_MAA = 368
        SM_UFA = 496
        SM_FLG = 512
        SM_GAM = 520
        SM_GH = 528
        SM_NGH = 536
        SM_SC = 544
        SM_SC2 = 560
        SM_EPS = 576
        SM_ONE = 579
        SM_T0 = 580
        SM_TX1 = 592
        SM_TX2 = 600

        Aview = A[:].rearrange("p (ic t) -> p ic t", t=2048)
        Bview = Bu[:].rearrange("p (ic t) -> p ic t", t=2048)

        tA0 = dma(SM[:, SM_A:SM_A + 256].rearrange("p (ic n) -> p ic n", n=16), a_log.rearrange("(ic p) n -> p ic n", p=128))
        tCW = dma(SM[:, SM_CONVW:SM_CONVW + 64].rearrange("p (ic k) -> p ic k", k=KC), convw.rearrange("(ic p) k -> p ic k", p=128))
        tCB = dma(SM[:, SM_CONVB:SM_CONVB + 16], convb)
        tDB = dma(SM[:, SM_DTB:SM_DTB + 16], dt_bias)
        tDV = dma(SM[:, SM_DV:SM_DV + 16], d_vec)
        tMA = dma(SM[:, SM_MAA:SM_MAA + 128], maa_c)
        tUF = dma(SM[:, SM_UFA:SM_UFA + 16], u_faa)
        tFL = dma(SM[:, SM_FLG:SM_FLG + 2], bass.AP(cflag.tensor, 0, [[0, 128], [1, 2]]))
        tID = dma(IDB[:], identb)
        tMS = dma(MSK[:], maskst)
        tIF = dma(IDF[:], identf)
        tZ = vec(lambda e: e.memset(ZRO[:], 0.0))
        tSMi = vec(lambda e: e.memset(SM[:, 514:768], 1.0))
        tE1 = vec(lambda e: e.memset(SM[:, SM_EPS:SM_EPS + 1], EPS_LN))
        tE2 = vec(lambda e: e.memset(SM[:, SM_EPS + 1:SM_EPS + 2], EPS_GN))
        tE3 = vec(lambda e: e.memset(SM[:, SM_EPS + 2:SM_EPS + 3], EPS_RMS))
        tONE = vec(lambda e: e.memset(SM[:, SM_ONE:SM_ONE + 1], 1.0))
        tAe = sca(lambda e: e.activation(out=SM[:, SM_A:SM_A + 256], in_=SM[:, SM_A:SM_A + 256],
                                         func=AF.Exp), after=[tA0])
        tAn = vec(lambda e: e.tensor_scalar(out=SM[:, SM_A:SM_A + 256], in0=SM[:, SM_A:SM_A + 256],
                                            scalar1=-1.0, scalar2=None, op0=ALU.mult), after=[tAe])

        TW = [(w_gate, "g0"), (w_val, "v0"), (w_out0, "o0"),
              (w_gv_v, "g1"), (w_gv_g, "v1"), (w_out1, "o1")]
        N_HALF = D * HH
        SMROW = SM[0:1, 640:768]
        GSUM = SM[0:1, 624:632]
        ACC = SM[:, 632:633]
        RTMP = SM[:, 633:634]
        gam_ready = None
        if PHASES & 10:
            WFh = [GB[:, 0:1024], GB[:, 1024:2048]]
            rWFh = [Rgn(), Rgn()]
            tacc = vec(lambda e: e.memset(ACC, 0.0))
            pp = 0
            for wi, (w_ap, _) in enumerate(TW):
                rows, cols = w_ap.shape
                for r0 in range(0, rows, 128):
                    for c0 in range(0, cols, 1024):
                        buf, rb = WFh[pp], rWFh[pp]
                        pp ^= 1
                        td = dma(buf, w_ap[r0:r0 + 128, c0:c0 + 1024],
                                 after=rb.bw(), eng="gpsimd")
                        rb.wrote(td)
                        tr = vec(lambda e, buf=buf: e.tensor_reduce(
                            out=RTMP, in_=buf, axis=AX.X, op=ALU.add,
                            apply_absolute_value=True), after=[td, tacc])
                        rb.read(tr)
                        tacc = vec(lambda e: e.tensor_add(ACC, ACC, RTMP), after=[tr])
                tt_ = ten(lambda e: e.transpose(PSC[0:1, 0:128], ACC, IDF[:]),
                          after=[tacc, tIF] + R["PSC"].bw())
                R["PSC"].wrote(tt_)
                tv_ = vec(lambda e: e.tensor_copy(SMROW[:, 0:128], PSC[0:1, 0:128]),
                          after=[tt_])
                R["PSC"].read(tv_)
                tq_ = vec(lambda e, wi=wi: e.tensor_reduce(
                    out=GSUM[:, wi:wi + 1], in_=SMROW[:, 0:128], axis=AX.X, op=ALU.add),
                    after=[tv_])
                tacc = vec(lambda e: e.memset(ACC, 0.0), after=[tq_])
            tgw = dma(ag_gam_src.ap(), GSUM, after=[tacc], eng="gpsimd")
            tgc = coll("AllGather", ALU.bypass, ag_gam_src.ap(), ag_gam_dst.ap(), PAIRS,
                       after=[tgw])
            gam_pre = [tgc, tSMi]
            def emit_gamma_tail(extra):
                tgr = dma(SM[:, 640:656], bass.AP(ag_gam_dst, 0, [[0, 128], [1, 16]]),
                          after=gam_pre + extra + [tSMi], eng="scalar")
                tgs1 = vec(lambda e: e.tensor_add(SM[:, SM_GAM:SM_GAM + 8], SM[:, 640:648],
                                                  SM[:, 648:656]), after=[tgr])
                tgs2 = vec(lambda e: e.tensor_scalar(out=SM[:, SM_GAM:SM_GAM + 8],
                                                     in0=SM[:, SM_GAM:SM_GAM + 8],
                                                     scalar1=1.0 / (2.0 * N_HALF), scalar2=None,
                                                     op0=ALU.mult), after=[tgs1])
                if DEBUG & 128:
                    tdbg3a = dma(dbg[133][:, 0:16], SM[:, 640:656], after=[tgs2], eng="gpsimd")
                    dma(dbg[133][:, 16:24], SM[:, SM_GAM:SM_GAM + 8], after=[tgs2, tdbg3a], eng="gpsimd")
                tgh2 = vec(lambda e: e.tensor_scalar(out=SM[:, SM_GH:SM_GH + 8],
                                                     in0=SM[:, SM_GAM:SM_GAM + 8], scalar1=0.5,
                                                     scalar2=None, op0=ALU.mult), after=[tgs2])
                tgn = vec(lambda e: e.tensor_scalar(out=SM[:, SM_NGH:SM_NGH + 8],
                                                    in0=SM[:, SM_GH:SM_GH + 8], scalar1=-1.0,
                                                    scalar2=None, op0=ALU.mult), after=[tgh2])
                tgi1 = vec(lambda e: e.reciprocal(out=SM[:, 704:712], in_=SM[:, SM_GAM:SM_GAM + 8]),
                           after=[tgs2, tgn])
                return vec(lambda e: e.tensor_scalar(out=SM[:, 704:712], in0=SM[:, 704:712],
                                                     scalar1=2.0, scalar2=None, op0=ALU.mult),
                           after=[tgi1])
            gam_ready = None

        def ln_stats(x_ap, m_col, r_col, sq_ap, sq_rgn, eps_col, eps_tok, after=(), nfree=D):
            t1 = vec(lambda e: e.tensor_reduce(out=m_col, in_=x_ap, axis=AX.X, op=ALU.add),
                     after=after)
            t2 = sca(lambda e: e.activation(out=sq_ap, in_=x_ap, func=AF.Square),
                     after=list(after) + sq_rgn.bw())
            sq_rgn.wrote(t2)
            t3 = vec(lambda e: e.tensor_reduce(out=r_col, in_=sq_ap, axis=AX.X, op=ALU.add),
                     after=[t2])
            sq_rgn.read(t3)
            t4 = vec(lambda e: e.tensor_scalar(out=m_col, in0=m_col, scalar1=1.0 / nfree,
                                               scalar2=None, op0=ALU.mult), after=[t1])
            t5 = vec(lambda e: e.tensor_scalar(out=r_col, in0=r_col, scalar1=1.0 / nfree,
                                               scalar2=None, op0=ALU.mult), after=[t3])
            t6 = vec(lambda e: e.scalar_tensor_tensor(out=r_col, in0=m_col, scalar=m_col,
                                                      in1=r_col, op0=ALU.mult, op1=ALU.subtract),
                     after=[t4, t5])
            t7 = sca(lambda e: e.activation(out=r_col, in_=r_col, func=AF.Sqrt,
                                            scale=-1.0, bias=eps_col), after=[t6, eps_tok])
            t8 = vec(lambda e: e.reciprocal(out=r_col, in_=r_col), after=[t7])
            return t8

        p1_res = []
        if PHASES & 1:
            for tt in range(16):
                XT, rXT = C[:, 0:2048], R["C0"]
                SQ, rSQ = C[:, 2048:4096], R["C1"]
                tx = dma(XT, xb[tt * 128:(tt + 1) * 128, :], after=rXT.bw())
                rXT.wrote(tx)
                mcol = SM[:, SM_T0:SM_T0 + 1]
                rcol = SM[:, SM_T0 + 1:SM_T0 + 2]
                tst = ln_stats(XT, mcol, rcol, SQ, rSQ, SM[:, SM_EPS:SM_EPS + 1], tE1, after=[tx])
                rXT.read(tst)
                XLN, rXLN = Dp[:, 0:2048], R["Dp0"]
                tn = vec(lambda e, XT=XT, XLN=XLN, mcol=mcol, rcol=rcol: e.tensor_scalar(
                    out=XLN, in0=XT, scalar1=mcol, scalar2=rcol,
                    op0=ALU.subtract, op1=ALU.mult), after=[tst] + rXLN.bw())
                rXLN.wrote(tn)
                rXT.read(tn)
                for g in range(4):
                    trs = []
                    for j in range(4):
                        ic = g * 4 + j
                        trs.append(ten(lambda e, ic=ic, j=j, XLN=XLN: e.transpose(
                            PT[:, j * 128:(j + 1) * 128], XLN[:, ic * 128:(ic + 1) * 128],
                            IDB[:]), after=[tn, tID] + (R["PT"].bw() if j == 0 else [])))
                        rXLN.read(trs[-1])
                    R["PT"].wrote(trs[-1])
                    tcp = vec(lambda e, g=g, tt=tt: e.tensor_copy(
                        Aview[:, g * 4:(g + 1) * 4, tt * 128:tt * 128 + 128],
                        PT[:].rearrange("p (j t) -> p j t", j=4)),
                        after=trs + R["A"].bw())
                    R["PT"].read(tcp)
                    R["A"].wrote(tcp)
            tXLNT = R["A"].w
            if DEBUG & 8:
                R["A"].read(dma(dbg[8], A[:], after=[tXLNT], eng="gpsimd"))

            for (wsrc, is_gate) in ((w_inx, False), (w_ing, True)):
                for it in range(16):
                    twf = dma(WF[:, 0:2048].rearrange("p (k i) -> p k i", k=16),
                              wsrc[:, it * 128:(it + 1) * 128].rearrange("(k p) i -> p k i", p=128),
                              after=R["WF"].bw())
                    R["WF"].wrote(twf)
                    twb = sca(lambda e: e.activation(out=WB[:], in_=WF[:, 0:2048], func=AF.Copy),
                              after=[twf] + R["WB"].bw())
                    R["WF"].read(twb)
                    R["WB"].wrote(twb)
                    if not is_gate:
                        t_tail = vec(lambda e: e.memset(SMB[:, 384:387], 0.0),
                                     after=R["SMB0"].bw())
                        R["SMB0"].wrote(t_tail)
                    for tg in range(4):
                        pmi = tg % 2
                        pm, rpm = PM[pmi], PMr[pmi]
                        first_deps = [twb, tXLNT] + rpm.bw()
                        last_mm = None
                        for k in range(16):
                            last_mm = mm(pm[:], WB[:, k * 128:(k + 1) * 128],
                                         Aview[:, k, tg * 512:tg * 512 + 512],
                                         start=(k == 0), stop=(k == 15),
                                         after=(first_deps if k == 0 else []))
                        rpm.wrote(last_mm)
                        R["WB"].read(last_mm)
                        if is_gate:
                            sl = Dp[:, 2048 + (tg % 2) * 512: 2048 + (tg % 2) * 512 + 512]
                            rsl = R["Dp1"]
                            tv = vec(lambda e, pm=pm, sl=sl: e.tensor_copy(sl, pm[:]),
                                     after=[last_mm] + rsl.bw())
                            rpm.read(tv)
                            rsl.wrote(tv)
                            td = dma(gate_scr.ap()[:, it * 2048 + tg * 512:
                                                   it * 2048 + tg * 512 + 512],
                                     sl, after=[tv], eng="gpsimd")
                            rsl.read(td)
                        else:
                            ACCV, rACC = C[:, 4096:4608], R["C2"]
                            wc = [SM[:, SM_CONVW + it * 4 + q:SM_CONVW + it * 4 + q + 1]
                                  for q in range(4)]
                            TLB = SMB[:, 384:387]
                            c1 = vec(lambda e, pm=pm, w=wc[3]: e.tensor_scalar(
                                out=ACCV, in0=pm[:], scalar1=w, scalar2=None, op0=ALU.mult),
                                after=[last_mm, tCW] + rACC.bw())
                            rpm.read(c1)
                            c2 = vec(lambda e, pm=pm, w=wc[2]: e.scalar_tensor_tensor(
                                out=ACCV[:, 1:512], in0=pm[:, 0:511], scalar=w,
                                in1=ACCV[:, 1:512], op0=ALU.mult, op1=ALU.add), after=[c1])
                            c3 = vec(lambda e, pm=pm, w=wc[1]: e.scalar_tensor_tensor(
                                out=ACCV[:, 2:512], in0=pm[:, 0:510], scalar=w,
                                in1=ACCV[:, 2:512], op0=ALU.mult, op1=ALU.add), after=[c2])
                            c4 = vec(lambda e, pm=pm, w=wc[0]: e.scalar_tensor_tensor(
                                out=ACCV[:, 3:512], in0=pm[:, 0:509], scalar=w,
                                in1=ACCV[:, 3:512], op0=ALU.mult, op1=ALU.add), after=[c3])
                            b1 = vec(lambda e, w=wc[0]: e.scalar_tensor_tensor(
                                out=ACCV[:, 0:3], in0=TLB[:, 0:3], scalar=w,
                                in1=ACCV[:, 0:3], op0=ALU.mult, op1=ALU.add),
                                after=[c4, t_tail])
                            b2 = vec(lambda e, w=wc[1]: e.scalar_tensor_tensor(
                                out=ACCV[:, 0:2], in0=TLB[:, 1:3], scalar=w,
                                in1=ACCV[:, 0:2], op0=ALU.mult, op1=ALU.add), after=[b1])
                            b3 = vec(lambda e, w=wc[2]: e.scalar_tensor_tensor(
                                out=ACCV[:, 0:1], in0=TLB[:, 2:3], scalar=w,
                                in1=ACCV[:, 0:1], op0=ALU.mult, op1=ALU.add), after=[b2])
                            rACC.wrote(b3)
                            t_tail = vec(lambda e, pm=pm: e.tensor_copy(TLB[:, 0:3],
                                                                       pm[:, 509:512]),
                                         after=[b3, last_mm])
                            rpm.read(t_tail)
                            R["SMB0"].wrote(t_tail)
                            cb = SM[:, SM_CONVB + it:SM_CONVB + it + 1]
                            ts_ = sca(lambda e, it=it, tg=tg, cb=cb: e.activation(
                                out=Bview[:, it, tg * 512:tg * 512 + 512],
                                in_=ACCV, func=AF.Silu, bias=cb),
                                after=[b3, tCB] + R["Bu"].bw())
                            rACC.read(ts_)
                            R["Bu"].wrote(ts_)
            tXC = R["Bu"].w
            if DEBUG & 16:
                R["Bu"].read(dma(dbg[16], Bu[:], after=[tXC], eng="gpsimd"))

            txp1 = dma(WF[:, 0:1280].rearrange("p (k r) -> p k r", k=8),
                       w_xp[0:1024, :].rearrange("(k p) r -> p k r", p=128),
                       after=R["WF"].bw())
            R["WF"].wrote(txp1)
            txc1 = sca(lambda e: e.activation(out=WB[:, 0:1280], in_=WF[:, 0:1280],
                                              func=AF.Copy), after=[txp1] + R["WB"].bw())
            R["WF"].read(txc1)
            R["WB"].wrote(txc1)
            txp2 = dma(WF[:, 0:1280].rearrange("p (k r) -> p k r", k=8),
                       w_xp[1024:2048, :].rearrange("(k p) r -> p k r", p=128),
                       after=R["WF"].bw())
            R["WF"].wrote(txp2)
            txc2 = sca(lambda e: e.activation(out=WDT[:, 0:1280], in_=WF[:, 0:1280],
                                              func=AF.Copy), after=[txp2] + R["WDT"].bw())
            R["WF"].read(txc2)
            R["WDT"].wrote(txc2)
            for tg in range(4):
                rc = tg * 512
                lastr = lastbc = None
                for k in range(16):
                    wsl = WB if k < 8 else WDT
                    ko = (k % 8) * 160
                    deps = []
                    if k == 0:
                        deps = [txc1, txc2, tXC] + PMr[2].bw() + R["PSC"].bw()
                    lastr = mm(PM[2][:], wsl[:, ko:ko + 128],
                               Bview[:, k, rc:rc + 512],
                               start=(k == 0), stop=(k == 15), after=deps)
                    lastbc = mm(PSC[0:32, :], wsl[:, ko + 128:ko + 160],
                                Bview[:, k, rc:rc + 512],
                                start=(k == 0), stop=(k == 15), after=[])
                PMr[2].wrote(lastr)
                R["PSC"].wrote(lastbc)
                R["WB"].read(lastr)
                R["WDT"].read(lastr)
                tv1 = vec(lambda e, rc=rc: e.tensor_copy(C[:, rc:rc + 512], PM[2][:]),
                          after=[lastr] + (R["C0"].bw() + R["C1"].bw() if tg == 0 else []))
                PMr[2].read(tv1)
                if tg < 2:
                    R["C0"].wrote(tv1)
                else:
                    R["C1"].wrote(tv1)
                ts1 = sca(lambda e, rc=rc: e.activation(
                    out=Dp[:, rc:rc + 512], in_=C[:, rc:rc + 512], func=AF.Copy),
                    after=[tv1] + (R["Dp0"].bw() if tg == 0 else []))
                R["Dp0"].wrote(ts1)
                tv2 = vec(lambda e, rc=rc: e.tensor_copy(BCb[0:32, rc:rc + 512], PSC[0:32, :]),
                          after=[lastbc] + (R["BCb"].bw() if tg == 0 else []))
                R["PSC"].read(tv2)
                R["BCb"].wrote(tv2)
            tdti_f = R["C1"].w
            tdti_b = R["Dp0"].w
            tbc = dma(bc_scr.ap()[:, 0:2048], BCb[0:32, :], after=[R["BCb"].w], eng="gpsimd")
            R["BCb"].read(tbc)

            if DEBUG & 32:
                R["Dp0"].read(dma(dbg[32], Dp[:, 0:2048], after=[tdti_b], eng="gpsimd"))
                R["BCb"].read(dma(dbg[33], BCb[0:32, :], after=[R["BCb"].w], eng="gpsimd"))
            twd0 = dma(WF[:, 0:2048], w_dt, after=R["WF"].bw())
            R["WF"].wrote(twd0)
            twdt = sca(lambda e: e.activation(out=WDT[:], in_=WF[:, 0:2048], func=AF.Copy),
                       after=[twd0] + R["WDT"].bw())
            R["WF"].read(twdt)
            R["WDT"].wrote(twdt)

            rDT, rYA, rDA = R["C0"], R["C1"], R["C2"]
            for ic in range(16):
                DT = C[:, 0:2048]
                YA = C[:, 2048:4096]
                DA = C[:, 4096:6144]
                sp_toks = []
                for tg in range(4):
                    tmm = mm(PM[3][:], WDT[:, ic * 128:(ic + 1) * 128],
                             Dp[:, tg * 512:(tg + 1) * 512], start=True, stop=True,
                             after=[twdt, tdti_b] + PMr[3].bw())
                    PMr[3].wrote(tmm)
                    R["Dp0"].read(tmm)
                    R["WDT"].read(tmm)
                    Zb = C[:, 4096:4608]
                    Tb = C[:, 4608:5120]
                    z1 = vec(lambda e, ic=ic: e.tensor_scalar(
                        out=Zb, in0=PM[3][:], scalar1=SM[:, SM_DTB + ic:SM_DTB + ic + 1],
                        scalar2=None, op0=ALU.add),
                        after=[tmm, tDB] + rDA.bw())
                    PMr[3].read(z1)
                    z2 = sca(lambda e: e.activation(out=Tb, in_=Zb, func=AF.Abs), after=[z1])
                    z3 = sca(lambda e: e.activation(out=Tb, in_=Tb, func=AF.Exp, scale=-1.0),
                             after=[z2])
                    z4 = sca(lambda e: e.activation(out=Tb, in_=Tb, func=AF.Ln,
                                                    bias=SM[:, SM_ONE:SM_ONE + 1]),
                             after=[z3, tONE])
                    z5 = sca(lambda e: e.activation(out=Zb, in_=Zb, func=AF.Relu), after=[z4])
                    tsp = vec(lambda e, tg=tg: e.tensor_tensor(
                        out=DT[:, tg * 512:(tg + 1) * 512], in0=Zb, in1=Tb, op=ALU.add),
                        after=[z5] + (rDT.bw() if tg == 0 else []))
                    rDA.wrote(tsp)
                    sp_toks.append(tsp)
                rDT.wrote(sp_toks[-1])
                UB = QIb[:, 0:2048]
                tu = vec(lambda e, ic=ic: e.tensor_tensor(
                    out=UB, in0=DT, in1=Bview[:, ic, :], op=ALU.mult),
                    after=sp_toks + R["QI"].bw())
                rDT.read(tu)
                R["Bu"].read(tu)
                R["QI"].wrote(tu)
                tgl = dma(WB[:, 0:2048], gate_scr.ap()[:, ic * 2048:(ic + 1) * 2048],
                          after=R["WB"].bw())
                R["WB"].wrote(tgl)
                tgs = sca(lambda e: e.activation(out=WB[:, 0:2048], in_=WB[:, 0:2048],
                                                 func=AF.Silu), after=[tgl])
                R["WB"].wrote(tgs)
                tya = vec(lambda e: e.memset(YA, 0.0), after=rYA.bw())
                rYA.wrote(tya)
                for n in range(16):
                    acol = SM[:, SM_A + ic * 16 + n:SM_A + ic * 16 + n + 1]
                    tda = sca(lambda e, acol=acol: e.activation(
                        out=DA, in_=DT, func=AF.Exp, scale=acol),
                        after=[tAn] + rDA.bw())
                    rDT.read(tda)
                    rDA.wrote(tda)
                    tbr = dma(WFb[:, 0:2048], bass.AP(bc_scr, n * S, [[0, 128], [1, 2048]]),
                              after=[tbc] + R["WF"].bw())
                    R["WF"].wrote(tbr)
                    tcr = dma(WFb[:, 2048:4096], bass.AP(bc_scr, (16 + n) * S, [[0, 128], [1, 2048]]),
                              after=[tbc])
                    tdbu = vec(lambda e: e.tensor_tensor(out=QIb[:, 2048:4096], in0=UB,
                                                         in1=WFb[:, 0:2048], op=ALU.mult),
                               after=[tu, tbr])
                    R["WF"].read(tdbu)
                    tsc = vec(lambda e: e.tensor_tensor_scan(
                        out=DA, data0=DA, data1=QIb[:, 2048:4096], initial=0.0,
                        op0=ALU.mult, op1=ALU.add), after=[tda, tdbu])
                    rDA.wrote(tsc)
                    ttm = vec(lambda e: e.tensor_tensor(out=QIb[:, 2048:4096], in0=DA,
                                                        in1=WFb[:, 2048:4096], op=ALU.mult),
                              after=[tsc, tcr])
                    rDA.read(ttm)
                    R["WF"].read(ttm)
                    tyacc = vec(lambda e: e.tensor_tensor(out=YA, in0=YA,
                                                          in1=QIb[:, 2048:4096], op=ALU.add),
                                after=[ttm, tya])
                    rYA.wrote(tyacc)
                dvc = SM[:, SM_DV + ic:SM_DV + ic + 1]
                ty2 = vec(lambda e, ic=ic, dvc=dvc: e.scalar_tensor_tensor(
                    out=YA, in0=Bview[:, ic, :], scalar=dvc, in1=YA,
                    op0=ALU.mult, op1=ALU.add), after=[rYA.w, tDV])
                rYA.wrote(ty2)
                R["Bu"].read(ty2)
                tyf = vec(lambda e, ic=ic: e.tensor_tensor(
                    out=Aview[:, ic, :], in0=YA, in1=WB[:, 0:2048],
                    op=ALU.mult), after=[ty2, tgs] + R["A"].bw())
                rYA.read(tyf)
                R["WB"].read(tyf)
                R["A"].wrote(tyf)
            tYH = R["A"].w
            if DEBUG & 64:
                R["A"].read(dma(dbg[64], A[:], after=[tYH], eng="gpsimd"))

            s6_stores = []
            dp_slots = [Rgn() for _ in range(4)]
            for hb in range(4):
                for tgrp in range(4):
                    mm_last = [None] * 4
                    for k in range(16):
                        twf = dma(WF[:, 0:512], w_op[k * 128:(k + 1) * 128,
                                                     hb * 512:(hb + 1) * 512],
                                  after=R["WF"].bw())
                        R["WF"].wrote(twf)
                        twb = sca(lambda e: e.activation(out=WB[:, 0:512], in_=WF[:, 0:512],
                                                         func=AF.Copy),
                                  after=[twf] + R["WB"].bw())
                        R["WF"].read(twb)
                        R["WB"].wrote(twb)
                        for tt4 in range(4):
                            t0 = (tgrp * 4 + tt4) * 128
                            deps = [twb, tYH] + (PMr[tt4].bw() if k == 0 else [])
                            mm_last[tt4] = mm(PM[tt4][:],
                                              Aview[:, k, t0:t0 + 128],
                                              WB[:, 0:512], start=(k == 0), stop=(k == 15),
                                              after=deps)
                            if k == 0:
                                PMr[tt4].wrote(mm_last[tt4])
                        R["WB"].read(mm_last[3])
                    for tt4 in range(4):
                        PMr[tt4].wrote(mm_last[tt4])
                        t0 = (tgrp * 4 + tt4) * 128
                        sl = Dp[:, tt4 * 512:(tt4 + 1) * 512]
                        tv = vec(lambda e, tt4=tt4, sl=sl: e.tensor_copy(sl, PM[tt4][:]),
                                 after=[mm_last[tt4]] + dp_slots[tt4].bw())
                        PMr[tt4].read(tv)
                        dp_slots[tt4].wrote(tv)
                        td = dma(rs0_src.ap()[t0:t0 + 128, hb * 512:(hb + 1) * 512],
                                 sl, after=[tv], eng="gpsimd")
                        dp_slots[tt4].read(td)
                        s6_stores.append(td)
            R["Dp0"].wrote(s6_stores[-1])
            R["Dp1"].wrote(s6_stores[-1])

            trs0 = coll("ReduceScatter", ALU.add, rs0_src.ap(), rs0_dst.ap(), PAIRS,
                        after=s6_stores)

            for tt in range(8):
                XT, rXT = C[:, 0:2048], R["C0"]
                AT, rAT = Dp[:, 0:2048], R["Dp0"]
                trx = dma(XT, xsl[tt * 128:(tt + 1) * 128, :], after=rXT.bw())
                rXT.wrote(trx)
                tra = dma(AT, rs0_dst.ap()[tt * 128:(tt + 1) * 128, :],
                          after=[trs0] + rAT.bw())
                rAT.wrote(tra)
                tadd = vec(lambda e, XT=XT, AT=AT: e.tensor_tensor(
                    out=XT, in0=XT, in1=AT, op=ALU.add), after=[trx, tra])
                rXT.wrote(tadd)
                rAT.read(tadd)
                tst = dma(res_hbm.ap()[tt * 128:(tt + 1) * 128, :], XT,
                          after=[tadd], eng="gpsimd")
                rXT.read(tst)
                p1_res.append(tst)
                if DEBUG & 1:
                    p1_res.append(dma(dbg[1][tt * 128:(tt + 1) * 128, :], XT,
                                      after=[tadd], eng="gpsimd"))
            R["resH"].wrote(p1_res[-1])

        if (PHASES & 10) and gam_ready is None:
            gam_ready = emit_gamma_tail([R["A"].w] if R["A"].w else [])
        tern_done = {}
        if PHASES & 10:
            GBb = GB[:].bitcast(BF16)
            WFh2 = [GB[:, 0:1024]]
            rWFh2 = [rWFh[0]]
            WBh = [GBb[:, 2048:3072]]
            rWBh = [Rgn()]
            WDh = [GBb[:, 3072:4096]]
            rWDh = [rWFh[1]]
            pp = 0
            for wi, (w_ap, key) in enumerate(TW):
                rows, cols = w_ap.shape
                last = None
                for r0 in range(0, rows, 128):
                    for c0 in range(0, cols, 1024):
                        fb, rfb = WFh2[0], rWFh2[0]
                        bb, rbb = WBh[0], rWBh[0]
                        db, rdb = WDh[0], rWDh[0]
                        td = dma(fb, w_ap[r0:r0 + 128, c0:c0 + 1024],
                                 after=rfb.bw(), eng="scalar")
                        rfb.wrote(td)
                        s1_ = sca(lambda e, fb=fb, bb=bb, wi=wi: e.activation(
                            out=bb, in_=fb, func=AF.Sign,
                            bias=SM[:, SM_NGH + wi:SM_NGH + wi + 1]),
                            after=[td, gam_ready] + rbb.bw())
                        rfb.read(s1_)
                        rbb.wrote(s1_)
                        s2_ = sca(lambda e, fb=fb, db=db, wi=wi: e.activation(
                            out=db, in_=fb, func=AF.Sign,
                            bias=SM[:, SM_GH + wi:SM_GH + wi + 1]),
                            after=[td, gam_ready] + rdb.bw())
                        rfb.read(s2_)
                        rdb.wrote(s2_)
                        v1_ = vec(lambda e, bb=bb, db=db: e.tensor_add(bb, bb, db),
                                  after=[s1_, s2_])
                        rbb.wrote(v1_)
                        rdb.read(v1_)
                        last = dma(tern[key].ap()[r0:r0 + 128, c0:c0 + 1024], bb,
                                   after=[v1_], eng="gpsimd")
                        rbb.read(last)
                tern_done[key] = last


        def emit_ffn(l):
            if l == 0:
                k_silu, gi_silu = "g0", 0
                k_val, gi_val = "v0", 1
                k_out, gi_out = "o0", 2
                b_silu, b_valr, b_outr = b_gate, b_val, b_out0
            else:
                k_silu, gi_silu = "v1", 4
                k_val, gi_val = "g1", 3
                k_out, gi_out = "o1", 5
                b_silu = b_valr = b_outr = None
            SC = SM[:, SM_SC:SM_SC + 16]
            SC2 = SM[:, SM_SC2:SM_SC2 + 16]
            EVs = SM[:, 640:656]
            EVv = SM[:, 656:672]
            EVo = SM[:, 672:688]
            CGt = SM[:, 688:704]
            rSC, rSC2, rEV = Rgn(), Rgn(), Rgn()

            pre_done = []
            sc_store = []
            for tt in range(8):
                RT, rRT = C[:, 0:2048], R["C0"]
                XL, rXL = C[:, 2048:4096], R["C1"]
                SQ, rSQ = C[:, 4096:6144], R["C2"]
                trl = dma(RT, res_hbm.ap()[tt * 128:(tt + 1) * 128, :],
                          after=R["resH"].br() + rRT.bw())
                rRT.wrote(trl)
                mcol = SM[:, SM_T0:SM_T0 + 1]
                rcol = SM[:, SM_T0 + 1:SM_T0 + 2]
                tst = ln_stats(RT, mcol, rcol, SQ, rSQ, SM[:, SM_EPS:SM_EPS + 1], tE1,
                               after=[trl])
                rRT.read(tst)
                tn = vec(lambda e, RT=RT, XL=XL, mcol=mcol, rcol=rcol: e.tensor_scalar(
                    out=XL, in0=RT, scalar1=mcol, scalar2=rcol,
                    op0=ALU.subtract, op1=ALU.mult), after=[tst] + rXL.bw())
                rRT.read(tn)
                rXL.wrote(tn)
                if l == 1:
                    tl2 = dma(ln2_hbm.ap()[tt * 128:(tt + 1) * 128, :], XL, after=[tn],
                              eng="gpsimd")
                    rXL.read(tl2)
                    pre_done.append(tl2)
                    t2a = sca(lambda e, XL=XL, SQ=SQ: e.activation(out=SQ, in_=XL,
                                                                   func=AF.Square),
                              after=[tn] + rSQ.bw())
                    rSQ.wrote(t2a)
                    rXL.read(t2a)
                    t2b = vec(lambda e, SQ=SQ: e.tensor_reduce(
                        out=rcol, in_=SQ, axis=AX.X, op=ALU.add), after=[t2a])
                    rSQ.read(t2b)
                    t2c = vec(lambda e: e.tensor_scalar(out=rcol, in0=rcol,
                                                        scalar1=1.0 / D, scalar2=None,
                                                        op0=ALU.mult), after=[t2b])
                    t2d = sca(lambda e: e.activation(out=rcol, in_=rcol, func=AF.Sqrt,
                                                     bias=SM[:, SM_EPS + 2:SM_EPS + 3]),
                              after=[t2c, tE3])
                    t2e = vec(lambda e: e.reciprocal(out=rcol, in_=rcol), after=[t2d])
                    t2f = vec(lambda e, XL=XL: e.scalar_tensor_tensor(
                        out=XL, in0=XL, scalar=rcol, in1=WF[:, 0:2048],
                        op0=ALU.mult, op1=ALU.mult), after=[t2e, tNW])
                    rXL.wrote(t2f)
                scol = SC[:, 8 * 0 + tt:8 * 0 + tt + 1] if False else None
                scol = SM[:, SM_T0 + 2:SM_T0 + 3]
                tq1 = vec(lambda e, XL=XL, scol=scol: e.tensor_reduce(
                    out=scol, in_=XL, axis=AX.X, op=ALU.max,
                    apply_absolute_value=True), after=[rXL.w])
                rXL.read(tq1)
                tq2 = vec(lambda e, scol=scol: e.tensor_scalar(
                    out=scol, in0=scol, scalar1=1e-5, scalar2=None, op0=ALU.max),
                    after=[tq1])
                tq3 = vec(lambda e, scol=scol: e.reciprocal(out=scol, in_=scol),
                          after=[tq2])
                tq4 = vec(lambda e, scol=scol: e.tensor_scalar(
                    out=scol, in0=scol, scalar1=127.0, scalar2=None, op0=ALU.mult),
                    after=[tq3])
                ti1 = vec(lambda e, XL=XL, scol=scol: e.tensor_scalar(
                    out=QI[:], in0=XL, scalar1=scol, scalar2=None, op0=ALU.mult),
                    after=[tq4] + R["QI"].bw())
                R["QI"].wrote(ti1)
                rXL.read(ti1)
                ti2 = vec(lambda e, XL=XL, SQ=SQ, scol=scol: e.tensor_scalar(
                    out=SQ[:, 0:2048], in0=XL, scalar1=scol, scalar2=None, op0=ALU.mult),
                    after=[tq4] + rSQ.bw())
                rSQ.wrote(ti2)
                rXL.read(ti2)
                ti3 = vec(lambda e, SQ=SQ: e.tensor_tensor(
                    out=SQ[:, 0:2048], in0=SQ[:, 0:2048], in1=QI[:], op=ALU.subtract),
                    after=[ti2, ti1])
                R["QI"].read(ti3)
                rSQ.wrote(ti3)
                ti4 = vec(lambda e, RT=RT, SQ=SQ: e.tensor_scalar(
                    out=RT, in0=SQ[:, 0:2048], scalar1=0.5, scalar2=None, op0=ALU.is_gt),
                    after=[ti3] + rRT.bw())
                rSQ.read(ti4)
                rRT.wrote(ti4)
                ti5 = vec(lambda e, RT=RT, SQ=SQ: e.scalar_tensor_tensor(
                    out=RT, in0=SQ[:, 0:2048], scalar=-0.5, in1=RT,
                    op0=ALU.is_lt, op1=ALU.subtract), after=[ti4])
                rRT.wrote(ti5)
                rSQ.read(ti5)
                XQ, rXQ = Dp[:, 0:2048], R["Dp0"]
                ti6 = vec(lambda e, RT=RT, XQ=XQ: e.tensor_tensor(
                    out=XQ, in0=QI[:], in1=RT, op=ALU.subtract),
                    after=[ti5] + rXQ.bw())
                R["QI"].read(ti6)
                rRT.read(ti6)
                rXQ.wrote(ti6)
                for g in range(4):
                    trs = []
                    for j in range(4):
                        ic = g * 4 + j
                        trs.append(ten(lambda e, ic=ic, j=j, XQ=XQ: e.transpose(
                            PT[:, j * 128:(j + 1) * 128], XQ[:, ic * 128:(ic + 1) * 128],
                            IDB[:]), after=[ti6, tID] + (R["PT"].bw() if j == 0 else [])))
                        rXQ.read(trs[-1])
                    R["PT"].wrote(trs[-1])
                    dstv = A[:].rearrange("p (ic t) -> p ic t", t=1024)
                    tcp = vec(lambda e, g=g, tt=tt, dstv=dstv: e.tensor_copy(
                        dstv[:, g * 4:(g + 1) * 4, tt * 128:tt * 128 + 128],
                        PT[:].rearrange("p (j t) -> p j t", j=4)),
                        after=trs + R["A"].bw())
                    R["PT"].read(tcp)
                    R["A"].wrote(tcp)
                    pre_done.append(tcp)
                myt = vec(lambda e, scol=scol, tt=tt: e.tensor_copy(
                    SM[:, SM_SC + tt:SM_SC + tt + 1],
                    scol), after=[tq4] + (rSC.bw() if tt == 0 else []))
                rSC.wrote(myt)
                tsc = dma(agsc_src[l].ap()[0:1, tt * 128:tt * 128 + 128].rearrange("o t -> t o"),
                          SM[:, SM_SC + tt:SM_SC + tt + 1], after=[myt], eng="gpsimd")
                rSC.read(tsc)
                sc_store.append(tsc)
            if FFN_PRE_ONLY:
                R["resH"].wrote(pre_done[-1] if pre_done else R["resH"].w)
                return
            txh = dma(agx_src[l].ap(), A[:, 0:16384], after=[R["A"].w], eng="gpsimd")
            R["A"].read(txh)
            tagx = coll("AllGather", ALU.bypass, agx_src[l].ap(), agx_dst[l].ap(), PAIRS,
                        after=[txh] + pre_done)
            tags = coll("AllGather", ALU.bypass, agsc_src[l].ap(), agsc_dst[l].ap(), PAIRS,
                        after=sc_store)
            tla = dma(A[:, 0:16384], agx_dst[l].ap()[0], after=[tagx] + R["A"].bw())
            R["A"].wrote(tla)
            tlb = dma(A[:, 16384:32768], agx_dst[l].ap()[1], after=[tagx])
            R["A"].wrote(tlb)
            tls = None
            for tt in range(16):
                tls = dma(SM[:, SM_SC + tt:SM_SC + tt + 1],
                          bass.AP(agsc_dst[l], tt * 128, [[1, 128], [1, 1]]),
                          after=[tags, tSMi] + ([tls] if tls else rSC.bw()))
            rSC.wrote(tls)
            tXQT = tlb

            tcg1 = vec(lambda e: e.tensor_scalar(
                out=CGt, in0=SC, scalar1=SM[:, 704 + gi_silu:705 + gi_silu],
                scalar2=None, op0=ALU.mult), after=[tls, gam_ready] + rEV.bw())
            tev1 = vec(lambda e: e.reciprocal(out=EVs, in_=CGt), after=[tcg1])
            tcg2 = vec(lambda e: e.tensor_scalar(
                out=CGt, in0=SC, scalar1=SM[:, 704 + gi_val:705 + gi_val],
                scalar2=None, op0=ALU.mult), after=[tev1])
            tev2 = vec(lambda e: e.reciprocal(out=EVv, in_=CGt), after=[tcg2])
            rEV.wrote(tev2)
            rSC.read(tev2)
            rowtok = {}
            if l == 0:
                for nm, gi, slot in (("s", gi_silu, 0), ("v", gi_val, 1)):
                    tcg = vec(lambda e, gi=gi: e.tensor_scalar(
                        out=CGt, in0=SC, scalar1=SM[:, 704 + gi:705 + gi],
                        scalar2=None, op0=ALU.mult), after=[tls, gam_ready, tev2])
                    tcb_ = vec(lambda e: e.tensor_copy(SMB[:, 0:16], CGt),
                               after=[tcg] + R["SMB1"].bw())
                    R["SMB1"].wrote(tcb_)
                    tsh = None
                    for tt in range(16):
                        tsh = dma(bass.AP(scrow_hbm, slot * S + tt * 128, [[1, 128], [1, 1]]),
                                  SMB[:, tt:tt + 1], after=[tcb_], eng="gpsimd")
                        R["SMB1"].read(tsh)
                    rowtok[nm] = tsh
            rWBa, rWBb = Rgn(), Rgn()
            rWBa.wrote(R["WB"].w)
            rWBb.wrote(R["WB"].w)
            h_done = []
            rs_stores = []
            for hf in range(2):
                hf_tiles = list(range(hf * 8, hf * 8 + 8))
                for part, (keyw, EV, bias_in, slot) in enumerate(
                        ((k_silu, EVs, b_silu, 0), (k_val, EVv, b_valr, 1))):
                    trow = None
                    if l == 0:
                        trow = dma(ROWB[0:1, 0:2048],
                                   bass.AP(scrow_hbm, slot * S, [[1, 1], [1, 2048]]),
                                   after=[rowtok["s" if part == 0 else "v"]] + R["ROWB"].bw())
                        R["ROWB"].wrote(trow)
                    for ob in range(8):
                        tbias = None
                        if l == 0:
                            tb1 = dma(C[0:1, 5632:6144], bias_in[0:1, ob * 512:(ob + 1) * 512],
                                      after=R["C2"].bw())
                            R["C2"].wrote(tb1)
                            tbias = vec(lambda e: e.tensor_copy(QIb[0:1, 0:512],
                                                                C[0:1, 5632:6144]),
                                        after=[tb1] + R["QI"].bw())
                            R["C2"].read(tbias)
                            R["QI"].wrote(tbias)
                        for tgi in range(2):
                            tg_tiles = hf_tiles[tgi * 4:tgi * 4 + 4]
                            mm_last = [None] * 4
                            for q in range(4):
                                gt = tg_tiles[q]
                                deps = PMr[q].bw() + [tXQT]
                                if l == 0:
                                    mm_last[q] = mm(
                                        PM[q][:], ROWB[0:1, gt * 128:gt * 128 + 128],
                                        QIb[0:1, 0:512], start=True, stop=False,
                                        after=deps + [trow, tbias])
                                    R["ROWB"].read(mm_last[q])
                                    R["QI"].read(mm_last[q])
                            for k in range(16):
                                wb_h = WB[:, (k % 2) * 1024:(k % 2) * 1024 + 512]
                                rwb = rWBa if k % 2 == 0 else rWBb
                                tw = dma(wb_h, tern[keyw].ap()[k * 128:(k + 1) * 128,
                                                               ob * 512:(ob + 1) * 512],
                                         after=rwb.bw() + [tern_done[keyw]])
                                rwb.wrote(tw)
                                for q in range(4):
                                    gt = tg_tiles[q]
                                    st_ = (k == 0 and l != 0)
                                    mm_last[q] = mm(
                                        PM[q][:],
                                        A[:].rearrange("p (s ic t) -> p s ic t", s=2, t=1024)
                                        [:, gt // 8, k, (gt % 8) * 128:(gt % 8) * 128 + 128],
                                        wb_h, start=st_, stop=(k == 15),
                                        after=([tw] + (PMr[q].bw() + [tXQT] if st_ else [])))
                                    rwb.read(mm_last[q])
                                if k == 0:
                                    for q in range(4):
                                        PMr[q].wrote(mm_last[q])
                            for q in range(4):
                                gt = tg_tiles[q]
                                PMr[q].wrote(mm_last[q])
                                ev = EV[:, gt:gt + 1]
                                hsl = Bu[:].rearrange("p (tl o) -> p tl o", o=4096)[
                                    :, gt % 8, :]
                                if part == 0:
                                    te_ = sca(lambda e, q=q, ev=ev, hsl=hsl, ob=ob:
                                              e.activation(out=hsl[:, ob * 512:(ob + 1) * 512],
                                                           in_=PM[q][:], func=AF.Silu,
                                                           scale=ev),
                                              after=[mm_last[q], tev2] + R["Bu"].bw())
                                    R["Bu"].wrote(te_)
                                else:
                                    tv_ = vec(lambda e, q=q, ev=ev: e.tensor_scalar(
                                        out=C[:, 4096:4608], in0=PM[q][:], scalar1=ev,
                                        scalar2=None, op0=ALU.mult),
                                        after=[mm_last[q], tev2] + R["C2"].bw())
                                    R["C2"].wrote(tv_)
                                    te_ = vec(lambda e, hsl=hsl, ob=ob: e.tensor_tensor(
                                        out=hsl[:, ob * 512:(ob + 1) * 512],
                                        in0=hsl[:, ob * 512:(ob + 1) * 512],
                                        in1=C[:, 4096:4608], op=ALU.mult),
                                        after=[tv_] + R["Bu"].br())
                                    R["C2"].read(te_)
                                    R["Bu"].wrote(te_)
                                PMr[q].read(te_)
                                h_done.append(te_)
                mx_st = []
                for q, gt in enumerate(hf_tiles):
                    mxc = SM[:, SM_SC2 + gt:SM_SC2 + gt + 1]
                    hsl = Bu[:].rearrange("p (tl o) -> p tl o", o=4096)[:, gt % 8, :]
                    tm1 = vec(lambda e, hsl=hsl, mxc=mxc: e.tensor_reduce(
                        out=mxc, in_=hsl, axis=AX.X, op=ALU.max,
                        apply_absolute_value=True),
                        after=[R["Bu"].w] + rSC2.bw())
                    R["Bu"].read(tm1)
                    rSC2.wrote(tm1)
                    tm2 = dma(armx_src[l][hf].ap()[0:1, (gt % 8) * 128:(gt % 8) * 128 + 128]
                              .rearrange("o t -> t o"), mxc, after=[tm1], eng="gpsimd")
                    rSC2.read(tm2)
                    mx_st.append(tm2)
                tagm = coll("AllGather", ALU.bypass, armx_src[l][hf].ap(),
                            armx_dst[l][hf].ap(), PAIRS, after=mx_st)
                tld1 = tld2 = None
                for q8 in range(8):
                    tld1 = dma(SM[:, SM_TX1 + q8:SM_TX1 + q8 + 1],
                               bass.AP(armx_dst[l][hf], q8 * 128, [[1, 128], [1, 1]]),
                               after=[tagm, tSMi] + ([tld2] if tld2 else []))
                    tld2 = dma(SM[:, SM_TX2 + q8:SM_TX2 + q8 + 1],
                               bass.AP(armx_dst[l][hf], TL + q8 * 128, [[1, 128], [1, 1]]),
                               after=[tld1])
                ts2a = vec(lambda e: e.tensor_tensor(
                    out=SM[:, SM_TX1:SM_TX1 + 8], in0=SM[:, SM_TX1:SM_TX1 + 8],
                    in1=SM[:, SM_TX2:SM_TX2 + 8], op=ALU.max), after=[tld1, tld2])
                ts2b = vec(lambda e: e.tensor_scalar(
                    out=SM[:, SM_TX1:SM_TX1 + 8], in0=SM[:, SM_TX1:SM_TX1 + 8],
                    scalar1=1e-5, scalar2=None, op0=ALU.max), after=[ts2a])
                ts2c = vec(lambda e: e.reciprocal(
                    out=SM[:, SM_TX1:SM_TX1 + 8], in_=SM[:, SM_TX1:SM_TX1 + 8]),
                    after=[ts2b])
                ts2d = vec(lambda e, hf=hf: e.tensor_scalar(
                    out=SC2[:, hf * 8:hf * 8 + 8], in0=SM[:, SM_TX1:SM_TX1 + 8],
                    scalar1=127.0, scalar2=None, op0=ALU.mult),
                    after=[ts2c] + (rSC2.bw() if hf == 0 else []))
                rSC2.wrote(ts2d)
                tcg3 = vec(lambda e: e.tensor_scalar(
                    out=CGt, in0=SC2, scalar1=SM[:, 704 + gi_out:705 + gi_out],
                    scalar2=None, op0=ALU.mult), after=[ts2d, gam_ready, tev2])
                tev3 = vec(lambda e: e.reciprocal(out=EVo, in_=CGt), after=[tcg3])
                rEV.wrote(tev3)
                trow_o = None
                if l == 0:
                    tcf = vec(lambda e: e.tensor_scalar(
                        out=CGt, in0=CGt, scalar1=SM[:, SM_FLG:SM_FLG + 1], scalar2=None,
                        op0=ALU.mult), after=[tev3, tFL])
                    tcb2 = vec(lambda e: e.tensor_copy(SMB[:, 0:16], CGt),
                               after=[tcf] + R["SMB1"].bw())
                    R["SMB1"].wrote(tcb2)
                    tsh2 = None
                    for tt in range(16):
                        tsh2 = dma(bass.AP(scrow_hbm, 2 * S + tt * 128, [[1, 128], [1, 1]]),
                                   SMB[:, tt:tt + 1], after=[tcb2], eng="gpsimd")
                        R["SMB1"].read(tsh2)
                    trow_o = dma(ROWB[0:1, 0:2048],
                                 bass.AP(scrow_hbm, 2 * S, [[1, 1], [1, 2048]]),
                                 after=[tsh2] + R["ROWB"].bw())
                    R["ROWB"].wrote(trow_o)
                hq_done = []
                for q, gt in enumerate(hf_tiles):
                    s2c = SC2[:, gt:gt + 1]
                    hsl = Bu[:].rearrange("p (tl o) -> p tl o", o=4096)[:, gt % 8, :]
                    for oc in range(2):
                        hv = hsl[:, oc * 2048:(oc + 1) * 2048]
                        tj1 = vec(lambda e, hv=hv, s2c=s2c: e.tensor_scalar(
                            out=QI[:], in0=hv, scalar1=s2c, scalar2=None, op0=ALU.mult),
                            after=[rSC2.w, R["Bu"].w] + R["QI"].bw())
                        R["QI"].wrote(tj1)
                        R["Bu"].read(tj1)
                        tj2 = vec(lambda e, hv=hv, s2c=s2c: e.tensor_scalar(
                            out=C[:, 2048:4096], in0=hv, scalar1=s2c, scalar2=None,
                            op0=ALU.mult), after=[rSC2.w] + R["C1"].bw())
                        R["C1"].wrote(tj2)
                        R["Bu"].read(tj2)
                        tj3 = vec(lambda e: e.tensor_tensor(
                            out=C[:, 2048:4096], in0=C[:, 2048:4096], in1=QI[:],
                            op=ALU.subtract), after=[tj2, tj1])
                        R["QI"].read(tj3)
                        R["C1"].wrote(tj3)
                        tj4 = vec(lambda e: e.tensor_scalar(
                            out=C[:, 4096:6144][:, 0:2048], in0=C[:, 2048:4096],
                            scalar1=0.5, scalar2=None, op0=ALU.is_gt),
                            after=[tj3] + R["C2"].bw())
                        R["C2"].wrote(tj4)
                        tj5 = vec(lambda e: e.scalar_tensor_tensor(
                            out=C[:, 4096:6144][:, 0:2048], in0=C[:, 2048:4096],
                            scalar=-0.5, in1=C[:, 4096:6144][:, 0:2048],
                            op0=ALU.is_lt, op1=ALU.subtract), after=[tj4])
                        R["C1"].read(tj5)
                        R["C2"].wrote(tj5)
                        tj6 = vec(lambda e, hv=hv: e.tensor_tensor(
                            out=hv, in0=QI[:], in1=C[:, 4096:6144][:, 0:2048],
                            op=ALU.subtract), after=[tj5] + R["Bu"].bw())
                        R["QI"].read(tj6)
                        R["C2"].read(tj6)
                        R["Bu"].wrote(tj6)
                        hq_done.append(tj6)
                for tgi in range(2):
                    tg_tiles = hf_tiles[tgi * 4:tgi * 4 + 4]
                    tr_done = []
                    for q in range(4):
                        gt = tg_tiles[q]
                        hsl = Bu[:].rearrange("p (tl o) -> p tl o", o=4096)[:, gt % 8, :]
                        for g in range(8):
                            trs = []
                            for j in range(4):
                                kc = g * 4 + j
                                trs.append(ten(lambda e, hsl=hsl, kc=kc, j=j: e.transpose(
                                    PT[:, j * 128:(j + 1) * 128],
                                    hsl[:, kc * 128:(kc + 1) * 128], IDB[:]),
                                    after=hq_done + [tID] + (R["PT"].bw() if j == 0 else [])))
                                R["Bu"].read(trs[-1])
                            R["PT"].wrote(trs[-1])
                            dstv = A[:].rearrange("p (q kc t) -> p q kc t", q=4, t=128)
                            tcp = vec(lambda e, q=q, g=g, dstv=dstv: e.tensor_copy(
                                dstv[:, q, g * 4:(g + 1) * 4, :],
                                PT[:].rearrange("p (j t) -> p j t", j=4)),
                                after=trs + R["A"].bw())
                            R["PT"].read(tcp)
                            R["A"].wrote(tcp)
                            tr_done.append(tcp)
                    thqt = tr_done[-1]
                    for dg in range(4):
                        mm_last = [None] * 4
                        tbias2 = None
                        if l == 0:
                            tb1 = dma(C[0:1, 5632:6144], b_outr[0:1, dg * 512:(dg + 1) * 512],
                                      after=R["C2"].bw())
                            R["C2"].wrote(tb1)
                            tbias2 = vec(lambda e: e.tensor_copy(QIb[0:1, 0:512],
                                                                 C[0:1, 5632:6144]),
                                         after=[tb1] + R["QI"].bw())
                            R["C2"].read(tbias2)
                            R["QI"].wrote(tbias2)
                            for q in range(4):
                                gt = tg_tiles[q]
                                mm_last[q] = mm(PM[q][:], ROWB[0:1, gt * 128:gt * 128 + 128],
                                                QIb[0:1, 0:512], start=True, stop=False,
                                                after=PMr[q].bw() + [trow_o, tbias2, thqt])
                                R["ROWB"].read(mm_last[q])
                                R["QI"].read(mm_last[q])
                        for k in range(32):
                            wb_h = WB[:, (k % 2) * 1024:(k % 2) * 1024 + 512]
                            rwb = rWBa if k % 2 == 0 else rWBb
                            tw = dma(wb_h, tern[k_out].ap()[k * 128:(k + 1) * 128,
                                                            dg * 512:(dg + 1) * 512],
                                     after=rwb.bw() + [tern_done[k_out]])
                            rwb.wrote(tw)
                            for q in range(4):
                                st_ = (k == 0 and l != 0)
                                mm_last[q] = mm(
                                    PM[q][:],
                                    A[:].rearrange("p (q2 kc t) -> p q2 kc t", q2=4, t=128)
                                    [:, q, k, :],
                                    wb_h, start=st_, stop=(k == 31),
                                    after=([tw, thqt] + (PMr[q].bw() if st_ else [])))
                                rwb.read(mm_last[q])
                            if k == 0:
                                for q in range(4):
                                    PMr[q].wrote(mm_last[q])
                        for q in range(4):
                            gt = tg_tiles[q]
                            PMr[q].wrote(mm_last[q])
                            sl = Dp[:, 2048 + q * 512:2048 + q * 512 + 512]
                            te_ = vec(lambda e, q=q, gt=gt, sl=sl: e.tensor_scalar(
                                out=sl, in0=PM[q][:], scalar1=EVo[:, gt:gt + 1],
                                scalar2=None, op0=ALU.mult),
                                after=[mm_last[q], tev3] + R["Dp1"].bw())
                            PMr[q].read(te_)
                            R["Dp1"].wrote(te_)
                            td_ = dma(rsf_src[l].ap()[gt * 128:(gt + 1) * 128,
                                                      dg * 512:(dg + 1) * 512],
                                      sl, after=[te_], eng="gpsimd")
                            R["Dp1"].read(td_)
                            rs_stores.append(td_)
            trsf = coll("ReduceScatter", ALU.add, rsf_src[l].ap(), rsf_dst[l].ap(), PAIRS,
                        after=rs_stores)
            res_toks = []
            for tt in range(8):
                RT, rRT = C[:, 0:2048], R["C0"]
                FT, rFT = Dp[:, 0:2048], R["Dp0"]
                trl = dma(RT, res_hbm.ap()[tt * 128:(tt + 1) * 128, :],
                          after=R["resH"].br() + rRT.bw())
                rRT.wrote(trl)
                trf = dma(FT, rsf_dst[l].ap()[tt * 128:(tt + 1) * 128, :],
                          after=[trsf] + rFT.bw())
                rFT.wrote(trf)
                ta1 = vec(lambda e, RT=RT, FT=FT: e.tensor_tensor(
                    out=RT, in0=RT, in1=FT, op=ALU.add), after=[trl, trf])
                rRT.wrote(ta1)
                rFT.read(ta1)
                if l == 1:
                    tll = dma(C[:, 2048:4096], ln2_hbm.ap()[tt * 128:(tt + 1) * 128, :],
                              after=pre_done + R["C1"].bw())
                    R["C1"].wrote(tll)
                    ta1 = vec(lambda e, RT=RT: e.tensor_tensor(
                        out=RT, in0=RT, in1=C[:, 2048:4096], op=ALU.add),
                        after=[ta1, tll])
                    R["C1"].read(ta1)
                    rRT.wrote(ta1)
                tst_ = dma(res_hbm.ap()[tt * 128:(tt + 1) * 128, :], RT,
                           after=[ta1], eng="gpsimd")
                rRT.read(tst_)
                res_toks.append(tst_)
                dbit = 2 if l == 0 else 4
                if DEBUG & dbit:
                    res_toks.append(dma(dbg[dbit][tt * 128:(tt + 1) * 128, :], RT,
                                        after=[ta1], eng="gpsimd"))
            R["resH"].wrote(res_toks[-1])
            R["WB"].wrote(rWBa.w)
            R["WB"].wrote(rWBb.w)

        def emit_rwkv():
            Av1 = A[:].rearrange("p (ic t) -> p ic t", t=1024)
            Bv1 = Bu[:].rearrange("p (ic t) -> p ic t", t=1024)

            def seeded(src):
                rg = Rgn()
                rg.w = src.w
                rg.readers = list(src.readers)
                return rg

            rA0, rA1 = seeded(R["A"]), seeded(R["A"])
            rB0, rB1 = seeded(R["Bu"]), seeded(R["Bu"])
            zD = [seeded(R["Dp0"]), seeded(R["Dp0"]), seeded(R["Dp1"]), seeded(R["Dp1"])]
            pPSC = [seeded(R["PSC"]), seeded(R["PSC"])]
            pD = [seeded(R["PSY"]), seeded(R["PSY"])]
            pDS = seeded(R["PSY"])
            pY2 = seeded(R["PSY"])
            pPM0 = PMr[0]
            pPT = R["PT"]

            def step(kind, fn, w=(), r=(), x=()):
                deps = list(x)
                for rg in r:
                    deps.extend(rg.br())
                for rg in w:
                    deps.extend(rg.bw())
                tok = kind(fn, after=deps)
                for rg in w:
                    rg.wrote(tok)
                for rg in r:
                    rg.read(tok)
                return tok

            pre_t = []
            for tt in range(8):
                RT, rRT = C[:, 0:2048], R["C0"]
                SQ, rSQ = C[:, 2048:4096], R["C1"]
                trl = dma(RT, res_hbm.ap()[tt * 128:(tt + 1) * 128, :],
                          after=R["resH"].br() + rRT.bw())
                rRT.wrote(trl)
                mcol = SM[:, SM_T0:SM_T0 + 1]
                rcol = SM[:, SM_T0 + 1:SM_T0 + 2]
                tst_ = ln_stats(RT, mcol, rcol, SQ, rSQ, SM[:, SM_EPS:SM_EPS + 1], tE1,
                                after=[trl])
                rRT.read(tst_)
                X1 = Dp[:, 0:2048]
                tn = vec(lambda e, RT=RT, X1=X1, mcol=mcol, rcol=rcol: e.tensor_scalar(
                    out=X1, in0=RT, scalar1=mcol, scalar2=rcol,
                    op0=ALU.subtract, op1=ALU.mult),
                    after=[tst_] + zD[0].bw() + zD[1].bw())
                rRT.read(tn)
                zD[0].wrote(tn)
                zD[1].wrote(tn)
                td = dma(ln1_hbm.ap()[1 + tt * 128:1 + (tt + 1) * 128, :], X1,
                         after=[tn], eng="gpsimd")
                zD[0].read(td)
                zD[1].read(td)
                pre_t.append(td)
                if tt == 7:
                    td2 = dma(agtok_src.ap(), X1[127:128, :], after=[tn], eng="gpsimd")
                    zD[0].read(td2)
                    zD[1].read(td2)
                    pre_t.append(td2)
            tagt = coll("AllGather", ALU.bypass, agtok_src.ap(), agtok_dst.ap(), PAIRS,
                        after=[pre_t[-1]])
            tbl = dma(ROWB[0:1, 0:2048], agtok_dst.ap()[0], after=[tagt] + R["ROWB"].bw())
            R["ROWB"].wrote(tbl)
            fcol0 = SM[0:1, SM_FLG:SM_FLG + 1]
            tbf = vec(lambda e, f=fcol0: e.tensor_scalar(
                out=ROWB[0:1, 0:2048], in0=ROWB[0:1, 0:2048], scalar1=f, scalar2=None,
                op0=ALU.mult), after=[tbl, tFL])
            R["ROWB"].wrote(tbf)
            tb0 = dma(ln1_hbm.ap()[0:1, :], ROWB[0:1, 0:2048], after=[tbf], eng="gpsimd")
            R["ROWB"].read(tb0)
            ln1_done = tb0

            x1xx_last = None
            for ic in range(16):
                tcu = None
                for q in range(8):
                    dsl = Dp[:, q * 128:(q + 1) * 128]
                    ssl = ln1_hbm.ap()[1 + q * 128:1 + (q + 1) * 128,
                                       ic * 128:(ic + 1) * 128]
                    tcu = E.op("sync", lambda e, d=dsl, s=ssl: e.dma_start(
                        out=d, in_=s, transpose=True),
                        after=([ln1_done] + zD[0].bw()) if q == 0 else [], dma=True)
                zD[0].wrote(tcu)
                tsh = None
                for q in range(8):
                    dsl = Dp[:, 1024 + q * 128:1024 + (q + 1) * 128]
                    ssl = ln1_hbm.ap()[q * 128:(q + 1) * 128, ic * 128:(ic + 1) * 128]
                    tsh = E.op("sync", lambda e, d=dsl, s=ssl: e.dma_start(
                        out=d, in_=s, transpose=True),
                        after=([ln1_done] + zD[1].bw()) if q == 0 else [], dma=True)
                zD[1].wrote(tsh)
                txx = vec(lambda e: e.tensor_sub(Dp[:, 2048:3072], Dp[:, 1024:2048],
                                                 Dp[:, 0:1024]),
                          after=[tcu, tsh] + zD[2].bw())
                zD[0].read(txx)
                zD[1].read(txx)
                zD[2].wrote(txx)
                mxcol = SM[:, SM_MAA + ic:SM_MAA + ic + 1]
                obsl = Bv1[:, ic, :]
                txxx = vec(lambda e, o=obsl, m=mxcol: e.scalar_tensor_tensor(
                    out=o, in0=Dp[:, 2048:3072], scalar=m, in1=Dp[:, 0:1024],
                    op0=ALU.mult, op1=ALU.add),
                    after=[txx, tcu, tMA] + (rB0.bw() if ic == 0 else []))
                zD[2].read(txxx)
                zD[0].read(txxx)
                rB0.wrote(txxx)
                t1 = dma(x1T_hbm.ap()[ic * 128:(ic + 1) * 128, :], Dp[:, 0:1024],
                         after=[txxx], eng="gpsimd")
                zD[0].read(t1)
                t2 = dma(xxT_hbm.ap()[ic * 128:(ic + 1) * 128, :], Dp[:, 2048:3072],
                         after=[txx], eng="gpsimd")
                zD[2].read(t2)
                x1xx_last = t2

            tm1a = dma(WF[:, 0:1280].rearrange("p (k r) -> p k r", k=8),
                       w_maa1[0:1024, :].rearrange("(k p) r -> p k r", p=128),
                       after=R["WF"].bw())
            R["WF"].wrote(tm1a)
            tm1ac = sca(lambda e: e.activation(out=WB[:, 0:1280], in_=WF[:, 0:1280],
                                               func=AF.Copy), after=[tm1a] + R["WB"].bw())
            R["WF"].read(tm1ac)
            R["WB"].wrote(tm1ac)
            tm1b = dma(WF[:, 0:1280].rearrange("p (k r) -> p k r", k=8),
                       w_maa1[1024:2048, :].rearrange("(k p) r -> p k r", p=128),
                       after=R["WF"].bw())
            R["WF"].wrote(tm1b)
            tm1bc = sca(lambda e: e.activation(out=WDT[:, 0:1280], in_=WF[:, 0:1280],
                                               func=AF.Copy), after=[tm1b] + R["WDT"].bw())
            R["WF"].read(tm1bc)
            R["WDT"].wrote(tm1bc)
            rT64 = Rgn()
            for _tok in tern_done.values():
                rT64.read(_tok)
            lora_done = []
            for f in range(5):
                evs = []
                for tg in range(2):
                    last = None
                    for k in range(16):
                        wsl = WB if k < 8 else WDT
                        ko = (k % 8) * 160 + f * 32
                        lhs = wsl[:, ko:ko + 32]
                        rhs = Bv1[:, k, tg * 512:(tg + 1) * 512]
                        deps = []
                        if k == 0:
                            deps = [tm1ac, tm1bc, rB0.w] + R["PSC"].bw()
                        last = mm(PSC[0:32, :], lhs, rhs, start=(k == 0), stop=(k == 15),
                                  after=deps)
                    R["PSC"].wrote(last)
                    rB0.read(last)
                    R["WB"].read(last)
                    R["WDT"].read(last)
                    osl = T32[0:32, tg * 512:(tg + 1) * 512]
                    te = sca(lambda e, o=osl: e.activation(out=o, in_=PSC[0:32, :],
                                                           func=AF.Tanh),
                             after=[last] + R["T32"].bw())
                    R["PSC"].read(te)
                    R["T32"].wrote(te)
                    evs.append(te)
                tdm = dma(tnh_hbm.ap()[f * 32:(f + 1) * 32, :], T32[0:32, 0:1024],
                          after=evs, eng="gpsimd")
                R["T32"].read(tdm)
                lora_done.append(tdm)

            def build_xT(jmaa, fidx):
                tw2 = dma(WF[0:32, 0:2048], w_maa2[fidx * 32:(fidx + 1) * 32, :],
                          after=R["WF"].bw())
                R["WF"].wrote(tw2)
                tw2c = sca(lambda e: e.activation(out=WDT[0:32, 0:2048],
                                                  in_=WF[0:32, 0:2048], func=AF.Copy),
                           after=[tw2] + R["WDT"].bw())
                R["WF"].read(tw2c)
                R["WDT"].wrote(tw2c)
                tth = dma(T32[0:32, 0:1024], tnh_hbm.ap()[fidx * 32:(fidx + 1) * 32, :],
                          after=[lora_done[-1]] + R["T32"].bw())
                R["T32"].wrote(tth)
                last_tb = None
                for ic in range(16):
                    t1 = dma(Dp[:, 0:1024], x1T_hbm.ap()[ic * 128:(ic + 1) * 128, :],
                             after=[x1xx_last] + zD[0].bw())
                    zD[0].wrote(t1)
                    t2 = dma(Dp[:, 1024:2048], xxT_hbm.ap()[ic * 128:(ic + 1) * 128, :],
                             after=[x1xx_last] + zD[1].bw())
                    zD[1].wrote(t2)
                    lhs = WDT[0:32, ic * 128:(ic + 1) * 128]
                    mc = SM[:, SM_MAA + jmaa * 16 + ic:SM_MAA + jmaa * 16 + ic + 1]
                    for tg in range(2):
                        rhs = T32[0:32, tg * 512:(tg + 1) * 512]
                        tmm = mm(PSC[:], lhs, rhs, start=True, stop=True,
                                 after=[tw2c, tth] + R["PSC"].bw())
                        R["PSC"].wrote(tmm)
                        R["WDT"].read(tmm)
                        R["T32"].read(tmm)
                        tm2 = vec(lambda e, mc=mc: e.tensor_scalar(
                            out=QIb[:, 3072:3584], in0=PSC[:], scalar1=mc, scalar2=None,
                            op0=ALU.add), after=[tmm, tMA] + R["QI"].bw())
                        R["PSC"].read(tm2)
                        R["QI"].wrote(tm2)
                        xxsl = Dp[:, 1024 + tg * 512:1024 + (tg + 1) * 512]
                        tp = vec(lambda e, xs=xxsl: e.tensor_tensor(
                            out=QIb[:, 3584:4096], in0=xs, in1=QIb[:, 3072:3584],
                            op=ALU.mult), after=[tm2, t2])
                        zD[1].read(tp)
                        R["QI"].wrote(tp)
                        x1sl = Dp[:, tg * 512:(tg + 1) * 512]
                        obsl = Bv1[:, ic, tg * 512:(tg + 1) * 512]
                        tb = vec(lambda e, o=obsl, xs=x1sl: e.tensor_tensor(
                            out=o, in0=xs, in1=QIb[:, 3584:4096], op=ALU.add),
                            after=[tp, t1] + (rB0.bw() if (ic == 0 and tg == 0) else []))
                        zD[0].read(tb)
                        R["QI"].read(tb)
                        rB0.wrote(tb)
                        last_tb = tb
                return last_tb

            def proj_fm(w_ap, ic0, xT_tok):
                last_ev = None
                for oc in range(16):
                    wsrc = w_ap[:, oc * 128:(oc + 1) * 128].rearrange(
                        "(k p) i -> p k i", p=128)
                    twf = dma(WF[:, 0:2048].rearrange("p (k i) -> p k i", k=16), wsrc,
                              after=R["WF"].bw())
                    R["WF"].wrote(twf)
                    twb = sca(lambda e: e.activation(out=WB[:], in_=WF[:, 0:2048],
                                                     func=AF.Copy),
                              after=[twf] + R["WB"].bw())
                    R["WF"].read(twb)
                    R["WB"].wrote(twb)
                    for tg in range(2):
                        pm, rpm = PM[tg], PMr[tg]
                        last = None
                        for k in range(16):
                            lhs = WB[:, k * 128:(k + 1) * 128]
                            rhs = Bv1[:, k, tg * 512:(tg + 1) * 512]
                            deps = []
                            if k == 0:
                                deps = [twb, xT_tok] + rpm.bw()
                            last = mm(pm[:], lhs, rhs, start=(k == 0), stop=(k == 15),
                                      after=deps)
                        rpm.wrote(last)
                        R["WB"].read(last)
                        rB0.read(last)
                        osl = Av1[:, ic0 + oc, tg * 512:(tg + 1) * 512]
                        dr = rA1 if ic0 else rA0
                        te = vec(lambda e, o=osl, pm=pm: e.tensor_copy(o, pm[:]),
                                 after=[last] + (dr.bw() if (oc == 0 and tg == 0) else []))
                        rpm.read(te)
                        dr.wrote(te)
                        last_ev = te
                return last_ev

            xw_tok = build_xT(1, 0)
            td1 = dma(WF[:, 0:1024].rearrange("p (k r) -> p k r", k=16),
                      w_dec1.rearrange("(k p) r -> p k r", p=128), after=R["WF"].bw())
            R["WF"].wrote(td1)
            td1c = sca(lambda e: e.activation(out=WB[:, 0:1024], in_=WF[:, 0:1024],
                                              func=AF.Copy), after=[td1] + R["WB"].bw())
            R["WF"].read(td1c)
            R["WB"].wrote(td1c)
            t64_toks = []
            for tg in range(2):
                last = None
                for k in range(16):
                    lhs = WB[:, k * 64:(k + 1) * 64]
                    rhs = Bv1[:, k, tg * 512:(tg + 1) * 512]
                    deps = []
                    if k == 0:
                        deps = [td1c, xw_tok] + R["PSC"].bw()
                    last = mm(PSC[0:64, :], lhs, rhs, start=(k == 0), stop=(k == 15),
                              after=deps)
                R["PSC"].wrote(last)
                R["WB"].read(last)
                rB0.read(last)
                osl = T64[0:64, tg * 512:(tg + 1) * 512]
                te = sca(lambda e, o=osl: e.activation(out=o, in_=PSC[0:64, :],
                                                       func=AF.Tanh),
                         after=[last] + rT64.bw())
                R["PSC"].read(te)
                rT64.wrote(te)
                t64_toks.append(te)
            td2_ = dma(WF[0:64, 0:2048], w_dec2, after=R["WF"].bw())
            R["WF"].wrote(td2_)
            td2c = sca(lambda e: e.activation(out=WDT[0:64, 0:2048], in_=WF[0:64, 0:2048],
                                              func=AF.Copy), after=[td2_] + R["WDT"].bw())
            R["WF"].read(td2c)
            R["WDT"].wrote(td2c)
            ew_last = None
            for oc in range(16):
                lhs = WDT[0:64, oc * 128:(oc + 1) * 128]
                bc = SM[:, SM_MAA + 6 * 16 + oc:SM_MAA + 6 * 16 + oc + 1]
                for tg in range(2):
                    rhs = T64[0:64, tg * 512:(tg + 1) * 512]
                    tmm = mm(PM[2][:], lhs, rhs, start=True, stop=True,
                             after=[td2c] + t64_toks + PMr[2].bw())
                    PMr[2].wrote(tmm)
                    R["WDT"].read(tmm)
                    rT64.read(tmm)
                    osl = Bv1[:, 16 + oc, tg * 512:(tg + 1) * 512]
                    te = sca(lambda e, o=osl, bc=bc: e.activation(
                        out=o, in_=PM[2][:], func=AF.Exp, bias=bc),
                        after=[tmm, tMA] + (rB1.bw() if (oc == 0 and tg == 0) else []))
                    PMr[2].read(te)
                    rB1.wrote(te)
                    ew_last = te

            xk_tok = build_xT(2, 1)
            k_tok = proj_fm(w_k, 16, xk_tok)
            xr_tok = build_xT(4, 3)
            r_tok = proj_fm(w_r, 0, xr_tok)
            xv_tok = build_xT(3, 2)
            v_last = None
            for tg4 in range(2):
                for og in range(4):
                    mm_last = [None] * 4
                    for k in range(16):
                        wsrc = w_v[k * 128:(k + 1) * 128, og * 512:(og + 1) * 512]
                        twf = dma(WF[:, 0:512], wsrc, after=R["WF"].bw())
                        R["WF"].wrote(twf)
                        twb = sca(lambda e: e.activation(out=WB[:, 0:512],
                                                         in_=WF[:, 0:512], func=AF.Copy),
                                  after=[twf] + R["WB"].bw())
                        R["WF"].read(twb)
                        R["WB"].wrote(twb)
                        for q in range(4):
                            ttk = tg4 * 4 + q
                            lhs = Bv1[:, k, ttk * 128:(ttk + 1) * 128]
                            deps = [twb] + (([xv_tok] + PMr[q].bw()) if k == 0 else [])
                            mm_last[q] = mm(PM[q][:], lhs, WB[:, 0:512],
                                            start=(k == 0), stop=(k == 15), after=deps)
                            rB0.read(mm_last[q])
                            if k == 0:
                                PMr[q].wrote(mm_last[q])
                        R["WB"].read(mm_last[3])
                    for q in range(4):
                        ttk = tg4 * 4 + q
                        PMr[q].wrote(mm_last[q])
                        sl = Dp[:, 2048 + q * 512:2048 + (q + 1) * 512]
                        zi = 2 if q < 2 else 3
                        te = vec(lambda e, sl=sl, pm=PM[q]: e.tensor_copy(sl, pm[:]),
                                 after=[mm_last[q]] + zD[zi].bw())
                        PMr[q].read(te)
                        zD[zi].wrote(te)
                        td = dma(v_hbm.ap()[ttk * 128:(ttk + 1) * 128,
                                            og * 512:(og + 1) * 512], sl,
                                 after=[te], eng="gpsimd")
                        zD[zi].read(td)
                        v_last = td
            xg_tok = build_xT(5, 4)

            CW = C[:, 0:1025]
            ZER = C[:, 1025:2049]
            SRUN = C[:, 2049:3073]
            SCM = C[:, 3073:3074]
            SCB = C[:, 3074:3202]
            DCOL = C[:, 3202:3204]
            ECOL = C[:, 3204:3207]
            RCF = C[:, 3207:3335]
            SP0 = C[:, 3335:4375]
            TST = C[:, 4375:5399]
            rCW, rZER, rSRUN, rSCM, rSCB = Rgn(), Rgn(), Rgn(), Rgn(), Rgn()
            rDC, rEC, rSP0, rTST = Rgn(), Rgn(), Rgn(), Rgn()
            cseed = R["C0"].bw() + R["C1"].bw() + R["C2"].bw()
            qFA, qGA, qRT, qKT, qKU, qPP = (seeded(R["QI"]) for _ in range(6))
            qS0, qS1, qYE, qK2, qK2T, qFC, qRC = (seeded(R["QI"]) for _ in range(7))

            tz1 = vec(lambda e: e.memset(ZER, 0.0), after=cseed)
            rZER.wrote(tz1)
            tz2 = vec(lambda e: e.memset(CW[:, 0:1], 0.0), after=cseed)
            rCW.wrote(tz2)
            tz3 = vec(lambda e: e.memset(SRUN, 0.0), after=cseed)
            rSRUN.wrote(tz3)
            tob = vec(lambda e: e.memset(SMB[:, 480:481], 1.0), after=R["SMB1"].bw())
            R["SMB1"].wrote(tob)

            y_last = None
            for hp in range(16):
                ewsl = Bv1[:, 16 + hp, :]
                tcw = step(vec, lambda e, ew=ewsl: e.tensor_tensor_scan(
                    out=CW[:, 1:1025], data0=ZER, data1=ew, initial=0.0,
                    op0=ALU.add, op1=ALU.subtract),
                    w=(rCW,), r=(rZER, rB1), x=[ew_last])
                esl = SM[:, SM_SC2 + hp:SM_SC2 + hp + 1]
                tesl = sca(lambda e, o=esl: e.activation(out=o, in_=CW[:, 1024:1025],
                                                         func=AF.Exp),
                           after=rCW.br() + (R["SMsc2"].bw() if hp == 0 else []))
                rCW.read(tesl)
                R["SMsc2"].wrote(tesl)
                tvl = None
                for k8 in range(8):
                    dsl = Dp[:, k8 * 128:(k8 + 1) * 128]
                    ssl = v_hbm.ap()[k8 * 128:(k8 + 1) * 128, hp * 128:(hp + 1) * 128]
                    tvl = dma(dsl, ssl,
                              after=([v_last] + zD[0].bw()) if k8 == 0 else [])
                zD[0].wrote(tvl)
                for k in range(8):
                    ck = k * 128
                    cw_t = CW[:, ck:ck + 128]
                    cw_t1 = CW[:, ck + 1:ck + 129]
                    cw_s = CW[:, ck:ck + 1]
                    cw_e = CW[:, ck + 128:ck + 129]
                    rsl = Av1[:, hp, ck:ck + 128]
                    ksl = Av1[:, 16 + hp, ck:ck + 128]
                    t1 = step(vec, lambda e, a=cw_s, b=cw_e: e.tensor_add(SCM, a, b),
                              w=(rSCM,), r=(rCW,))
                    t2 = step(vec, lambda e: e.tensor_scalar(
                        out=SCM, in0=SCM, scalar1=0.5, scalar2=None, op0=ALU.mult),
                        w=(rSCM,))
                    t3 = step(vec, lambda e, a=cw_t: e.tensor_scalar(
                        out=SCB, in0=a, scalar1=SCM, scalar2=None, op0=ALU.subtract),
                        w=(rSCB,), r=(rCW, rSCM))
                    t4 = step(sca, lambda e: e.activation(out=QIb[:, 0:128], in_=SCB,
                                                          func=AF.Exp),
                              w=(qFA,), r=(rSCB,))
                    t5 = step(vec, lambda e, a=cw_t1: e.tensor_scalar(
                        out=SCB, in0=a, scalar1=SCM, scalar2=None, op0=ALU.subtract),
                        w=(rSCB,), r=(rCW, rSCM))
                    t6 = step(sca, lambda e: e.activation(out=QIb[:, 128:256], in_=SCB,
                                                          func=AF.Exp, scale=-1.0),
                              w=(qGA,), r=(rSCB,))
                    t7 = step(vec, lambda e, a=rsl: e.tensor_tensor(
                        out=QIb[:, 256:384], in0=a, in1=QIb[:, 0:128], op=ALU.mult),
                        w=(qRT,), r=(qFA,), x=[r_tok])
                    rA0.read(t7)
                    t8 = step(vec, lambda e, a=ksl: e.tensor_tensor(
                        out=QIb[:, 384:512], in0=a, in1=QIb[:, 128:256], op=ALU.mult),
                        w=(qKT,), r=(qGA,), x=[k_tok])
                    rA1.read(t8)
                    ucol = SM[:, SM_UFA + hp:SM_UFA + hp + 1]
                    t9 = step(vec, lambda e, a=ksl, u=ucol: e.tensor_scalar(
                        out=QIb[:, 640:768], in0=a, scalar1=u, scalar2=None,
                        op0=ALU.mult), w=(qKU,), x=[tUF, k_tok])
                    rA1.read(t9)
                    t10 = step(vec, lambda e, a=rsl: e.tensor_tensor(
                        out=QIb[:, 768:896], in0=a, in1=QIb[:, 640:768], op=ALU.mult),
                        w=(qPP,), r=(qKU,), x=[r_tok])
                    rA0.read(t10)
                    sc_tok = []
                    for hh in range(2):
                        po = 64 * hh
                        psl = PSC[:, 128 * hh:128 * hh + 128]
                        lhsK = QIb[po:po + 64, 384:512]
                        rhsR = QIb[po:po + 64, 256:384]
                        tm_ = step(ten, lambda e, p=psl, l=lhsK, r_=rhsR: e.matmul(
                            p, l, r_, start=True, stop=True),
                            w=(pPSC[hh],), r=(qKT, qRT))
                        dsl = PSY[:, hh:hh + 1]
                        lhsP = QIb[po:po + 64, 768:896]
                        ones = SMB[po:po + 64, 480:481]
                        td_ = step(ten, lambda e, p=dsl, l=lhsP, o=ones: e.matmul(
                            p, l, o, start=True, stop=True),
                            w=(pD[hh],), r=(qPP,), x=[tob])
                        dcl = DCOL[:, hh:hh + 1]
                        tdc = step(vec, lambda e, d=dcl, s=dsl: e.tensor_copy(d, s),
                                   w=(rDC,), r=(pD[hh],))
                        ssel = QIb[:, 896 + 128 * hh:1024 + 128 * hh]
                        qS = qS0 if hh == 0 else qS1
                        tsel = step(vec, lambda e, o=ssel, p=psl: e.select(
                            o, MSK[:], p, ZRO[:]),
                            w=(qS,), r=(pPSC[hh],), x=[tMS, tZ])
                        tdg = step(vec, lambda e, o=ssel, d=dcl: e.scalar_tensor_tensor(
                            out=o, in0=IDF[:], scalar=d, in1=o,
                            op0=ALU.mult, op1=ALU.add),
                            w=(qS,), r=(rDC,), x=[tIF])
                        sc_tok.append(tdg)
                    t11 = step(vec, lambda e, a=cw_t, b=cw_s: e.tensor_scalar(
                        out=SCB, in0=a, scalar1=b, scalar2=None, op0=ALU.subtract),
                        w=(rSCB,), r=(rCW,))
                    t12 = step(sca, lambda e: e.activation(out=QIb[:, 1664:1792],
                                                           in_=SCB, func=AF.Exp),
                              w=(qFC,), r=(rSCB,))
                    t13 = step(vec, lambda e, a=rsl: e.tensor_tensor(
                        out=RCF, in0=a, in1=QIb[:, 1664:1792],
                        op=ALU.mult), w=(qRC,), r=(qFC,), x=[r_tok])
                    rA0.read(t13)
                    for hh in range(2):
                        po = 64 * hh
                        ysl = PM[0][:, po:po + 64]
                        lhsS = QIb[:, 896 + 128 * hh:1024 + 128 * hh]
                        v_h = Dp[:, ck + po:ck + po + 64]
                        qS = qS0 if hh == 0 else qS1
                        ty1 = step(ten, lambda e, p=ysl, l=lhsS, r_=v_h, st=(k == 0):
                                   e.matmul(p, l, r_, start=True, stop=st),
                                   w=(pPM0,), r=(qS,), x=[tvl, sc_tok[hh]])
                        zD[0].read(ty1)
                        if k > 0:
                            lhsRC = RCF[po:po + 64, :]
                            ssl2 = SRUN[po:po + 64, hp * 64:hp * 64 + 64]
                            step(ten, lambda e, p=ysl, l=lhsRC, r_=ssl2: e.matmul(
                                p, l, r_, start=False, stop=True),
                                w=(pPM0,), r=(qRC, rSRUN))
                    tev = step(vec, lambda e: e.tensor_copy(QIb[:, 1280:1408],
                                                            PM[0][:, 0:128]),
                               w=(qYE,), r=(pPM0,))
                    tys = dma(y_scr.ap()[ck:ck + 128, hp * 128:(hp + 1) * 128],
                              QIb[:, 1280:1408], after=[tev], eng="gpsimd")
                    qYE.read(tys)
                    y_last = tys
                    t14 = step(vec, lambda e, a=cw_e: e.tensor_sub(ECOL[:, 0:1], a, SCM),
                               w=(rEC,), r=(rCW, rSCM))
                    t15 = step(sca, lambda e: e.activation(out=ECOL[:, 0:1],
                                                           in_=ECOL[:, 0:1], func=AF.Exp),
                               w=(rEC,))
                    t16 = step(vec, lambda e: e.tensor_scalar(
                        out=QIb[:, 1408:1536], in0=QIb[:, 384:512],
                        scalar1=ECOL[:, 0:1], scalar2=None, op0=ALU.mult),
                        w=(qK2,), r=(qKT, rEC))
                    tt_ = step(ten, lambda e: e.transpose(PT[:, 0:128],
                                                          QIb[:, 1408:1536], IDB[:]),
                               w=(pPT,), r=(qK2,), x=[tID])
                    tk2 = step(vec, lambda e: e.tensor_copy(QIb[:, 1536:1664],
                                                            PT[:, 0:128]),
                               w=(qK2T,), r=(pPT,))
                    for hh in range(2):
                        po = 64 * hh
                        psl = PSY[po:po + 64, 128:192]
                        lhsK2 = QIb[:, 1536 + po:1600 + po]
                        v_h = Dp[:, ck + po:ck + po + 64]
                        tds = step(ten, lambda e, p=psl, l=lhsK2, r_=v_h: e.matmul(
                            p, l, r_, start=True, stop=True),
                            w=(pDS,), r=(qK2T,), x=[tvl])
                        zD[0].read(tds)
                    t17 = step(vec, lambda e, a=cw_e, b=cw_s: e.tensor_sub(
                        ECOL[:, 1:2], a, b), w=(rEC,), r=(rCW,))
                    t18 = step(sca, lambda e: e.activation(out=ECOL[:, 1:2],
                                                           in_=ECOL[:, 1:2], func=AF.Exp),
                               w=(rEC,))
                    ssl3 = SRUN[:, hp * 64:hp * 64 + 64]
                    step(vec, lambda e, o=ssl3: e.scalar_tensor_tensor(
                        out=o, in0=o, scalar=ECOL[:, 1:2], in1=PSY[:, 128:192],
                        op0=ALU.mult, op1=ALU.add),
                        w=(rSRUN,), r=(rEC, pDS))
            ts1 = dma(agst_src.ap()[:, 0:1024], SRUN, after=rSRUN.br(), eng="gpsimd")
            rSRUN.read(ts1)
            ts2 = dma(agst_src.ap()[:, 1024:1040], SM[:, SM_SC2:SM_SC2 + 16],
                      after=[R["SMsc2"].w, ts1], eng="gpsimd")
            R["SMsc2"].read(ts2)
            tastc = coll("AllGather", ALU.bypass, agst_src.ap(), agst_dst.ap(), PAIRS,
                         after=[ts1, ts2])

            tsp = dma(SP0, agst_dst.ap()[0], after=[tastc] + rSP0.bw())
            rSP0.wrote(tsp)
            ttst = dma(TST.rearrange("p (hp i) -> p hp i", i=64),
                       t_state.rearrange("(hp p) i -> p hp i", p=128),
                       after=cseed + rTST.bw())
            rTST.wrote(ttst)
            fcol = SM[:, SM_FLG:SM_FLG + 1]
            nfcol = SM[:, SM_FLG + 1:SM_FLG + 2]
            te2 = step(vec, lambda e: e.tensor_scalar(
                out=SCB[:, 0:16], in0=SP0[:, 1024:1040], scalar1=fcol, scalar2=nfcol,
                op0=ALU.mult, op1=ALU.add), w=(rSCB,), r=(rSP0,), x=[tFL])
            tsp2 = step(vec, lambda e: e.tensor_scalar(
                out=SP0[:, 0:1024], in0=SP0[:, 0:1024], scalar1=fcol, scalar2=None,
                op0=ALU.mult), w=(rSP0,), x=[tFL])
            for hp in range(16):
                ssl = SRUN[:, hp * 64:hp * 64 + 64]
                tsl = TST[:, hp * 64:hp * 64 + 64]
                psl = SP0[:, hp * 64:hp * 64 + 64]
                ecol = SCB[:, hp:hp + 1]
                step(vec, lambda e, o=ssl, a=tsl, p=psl, c=ecol: e.scalar_tensor_tensor(
                    out=o, in0=a, scalar=c, in1=p, op0=ALU.mult, op1=ALU.add),
                    w=(rSRUN,), r=(rTST, rSP0, rSCB))

            y2_last = None
            for hp in range(16):
                ewsl = Bv1[:, 16 + hp, :]
                tcw = step(vec, lambda e, ew=ewsl: e.tensor_tensor_scan(
                    out=CW[:, 1:1025], data0=ZER, data1=ew, initial=0.0,
                    op0=ALU.add, op1=ALU.subtract),
                    w=(rCW,), r=(rZER, rB1))
                for k in range(8):
                    ck = k * 128
                    cw_t = CW[:, ck:ck + 128]
                    tfg = step(sca, lambda e, a=cw_t: e.activation(
                        out=QIb[:, 0:128], in_=a, func=AF.Exp), w=(qFA,), r=(rCW,))
                    rsl = Av1[:, hp, ck:ck + 128]
                    trg = step(vec, lambda e, a=rsl: e.tensor_tensor(
                        out=RCF, in0=a, in1=QIb[:, 0:128], op=ALU.mult),
                        w=(qRC,), r=(qFA,), x=[r_tok])
                    rA0.read(trg)
                    for hh in range(2):
                        po = 64 * hh
                        psl = PSY[:, 256 + po:256 + po + 64]
                        lhsRG = RCF[po:po + 64, :]
                        ssl2 = SRUN[po:po + 64, hp * 64:hp * 64 + 64]
                        step(ten, lambda e, p=psl, l=lhsRG, r_=ssl2: e.matmul(
                            p, l, r_, start=True, stop=True),
                            w=(pY2,), r=(qRC, rSRUN))
                    tev = step(vec, lambda e: e.tensor_copy(QIb[:, 256:384],
                                                            PSY[:, 256:384]),
                               w=(qRT,), r=(pY2,))
                    td = dma(y2_scr.ap()[ck:ck + 128, hp * 128:(hp + 1) * 128],
                             QIb[:, 256:384], after=[tev], eng="gpsimd")
                    qRT.read(td)
                    y2_last = td

            if DEBUG & 1024:
                dma(dbg[1024], y_scr.ap(), after=[y_last], eng="gpsimd")
                dma(dbg[1025], y2_scr.ap(), after=[y2_last], eng="gpsimd")
                tdsin = dma(dbg[1026], SRUN, after=rSRUN.br(), eng="gpsimd")
                rSRUN.read(tdsin)
                tdew = dma(dbg[1027], Bu[:, 16384:32768], after=rB1.br(), eng="gpsimd")
                rB1.read(tdew)

            tlnw = dma(WF[:, 0:2048], bass.AP(lnw_row.tensor, 0, [[0, 128], [1, 2048]]),
                       after=R["WF"].bw())
            R["WF"].wrote(tlnw)
            rGB = Rgn()
            tlnb = dma(GB[:, 0:2048], bass.AP(lnb_row.tensor, 0, [[0, 128], [1, 2048]]),
                       after=list(tern_done.values()) + rGB.bw() + rT64.bw())
            rGB.wrote(tlnb)
            qall = [qFA, qGA, qRT, qKT, qKU, qPP, qS0, qS1, qYE, qK2, qK2T, qFC, qRC]
            YT = C[:, 0:2048]
            SQ2 = C[:, 2048:4096]
            mc = SM[:, SM_T0:SM_T0 + 1]
            vc = SM[:, SM_T0 + 1:SM_T0 + 2]
            gn_done = []
            for tt in range(8):
                deps = [y_last, y2_last] + R["QI"].bw()
                if tt == 0:
                    for rg in qall:
                        deps += rg.bw()
                tld1 = dma(QIb[:, 0:2048], y_scr.ap()[tt * 128:(tt + 1) * 128, :],
                           after=deps)
                R["QI"].wrote(tld1)
                tld2 = dma(QIb[:, 2048:4096], y2_scr.ap()[tt * 128:(tt + 1) * 128, :],
                           after=[tld1])
                R["QI"].wrote(tld2)
                ty = step(vec, lambda e: e.tensor_tensor(
                    out=YT, in0=QIb[:, 0:2048], in1=QIb[:, 2048:4096], op=ALU.add),
                    w=(rCW, rZER), x=[tld2])
                R["QI"].read(ty)
                tsq = step(sca, lambda e: e.activation(out=SQ2, in_=YT, func=AF.Square),
                           w=(rSRUN, rSCM, rSCB, rDC, rEC, rSP0, qRC), r=(rCW, rZER))
                prev = tsq
                for h in range(32):
                    ysl = YT[:, h * 64:(h + 1) * 64]
                    qsl = SQ2[:, h * 64:(h + 1) * 64]
                    s1 = vec(lambda e, a=ysl: e.tensor_reduce(out=mc, in_=a, axis=AX.X,
                                                              op=ALU.add),
                             after=[ty, prev])
                    s2 = vec(lambda e, a=qsl: e.tensor_reduce(out=vc, in_=a, axis=AX.X,
                                                              op=ALU.add),
                             after=[tsq, s1])
                    s3 = vec(lambda e: e.tensor_scalar(out=mc, in0=mc, scalar1=1.0 / 64,
                                                       scalar2=None, op0=ALU.mult),
                             after=[s2])
                    s4 = vec(lambda e: e.tensor_scalar(out=vc, in0=vc, scalar1=1.0 / 64,
                                                       scalar2=None, op0=ALU.mult),
                             after=[s3])
                    s5 = vec(lambda e: e.scalar_tensor_tensor(out=vc, in0=mc, scalar=mc,
                                                              in1=vc, op0=ALU.mult,
                                                              op1=ALU.subtract),
                             after=[s4])
                    s6 = sca(lambda e: e.activation(out=vc, in_=vc, func=AF.Sqrt,
                                                    scale=-1.0,
                                                    bias=SM[:, SM_EPS + 1:SM_EPS + 2]),
                             after=[s5, tE2])
                    s7 = vec(lambda e: e.reciprocal(out=vc, in_=vc), after=[s6])
                    s8 = vec(lambda e, a=ysl: e.tensor_scalar(out=a, in0=a, scalar1=mc,
                                                              scalar2=vc,
                                                              op0=ALU.subtract,
                                                              op1=ALU.mult),
                             after=[s7])
                    prev = s8
                ta1 = vec(lambda e: e.tensor_tensor(out=YT, in0=YT, in1=WF[:, 0:2048],
                                                    op=ALU.mult), after=[prev, tlnw])
                R["WF"].read(ta1)
                ta2 = vec(lambda e: e.tensor_tensor(out=YT, in0=YT, in1=GB[:, 0:2048],
                                                    op=ALU.add), after=[ta1, tlnb])
                rGB.read(ta2)
                rCW.wrote(ta2)
                rZER.wrote(ta2)
                ynsl = Bu[:, 16384 + tt * 2048:16384 + (tt + 1) * 2048]
                tyn = vec(lambda e, o=ynsl: e.tensor_copy(o, YT),
                          after=[ta2] + (rB1.bw() if tt == 0 else []))
                rCW.read(tyn)
                rZER.read(tyn)
                rB1.wrote(tyn)
                gn_done.append(tyn)

            g_last = None
            for tg4 in range(2):
                for og in range(4):
                    mm_last = [None] * 4
                    for k in range(16):
                        wsrc = w_g[k * 128:(k + 1) * 128, og * 512:(og + 1) * 512]
                        twf = dma(WF[:, 0:512], wsrc, after=R["WF"].bw())
                        R["WF"].wrote(twf)
                        twb = sca(lambda e: e.activation(out=WB[:, 0:512],
                                                         in_=WF[:, 0:512], func=AF.Copy),
                                  after=[twf] + R["WB"].bw())
                        R["WF"].read(twb)
                        R["WB"].wrote(twb)
                        for q in range(4):
                            ttk = tg4 * 4 + q
                            lhs = Bv1[:, k, ttk * 128:(ttk + 1) * 128]
                            deps = [twb] + (([xg_tok] + PMr[q].bw()) if k == 0 else [])
                            mm_last[q] = mm(PM[q][:], lhs, WB[:, 0:512],
                                            start=(k == 0), stop=(k == 15), after=deps)
                            rB0.read(mm_last[q])
                            if k == 0:
                                PMr[q].wrote(mm_last[q])
                        R["WB"].read(mm_last[3])
                    for q in range(4):
                        ttk = tg4 * 4 + q
                        PMr[q].wrote(mm_last[q])
                        ts_ = sca(lambda e, pm=PM[q]: e.activation(
                            out=QIb[:, 0:512], in_=pm[:], func=AF.Silu),
                            after=[mm_last[q]] + R["QI"].bw())
                        PMr[q].read(ts_)
                        R["QI"].wrote(ts_)
                        ynsl = Bu[:, 16384 + ttk * 2048 + og * 512:
                                   16384 + ttk * 2048 + og * 512 + 512]
                        tyg = vec(lambda e, a=ynsl: e.tensor_tensor(
                            out=QIb[:, 512:1024], in0=a, in1=QIb[:, 0:512],
                            op=ALU.mult), after=[ts_, gn_done[ttk]])
                        rB1.read(tyg)
                        R["QI"].wrote(tyg)
                        trs = []
                        for j in range(4):
                            tr = ten(lambda e, j=j: e.transpose(
                                PT[:, j * 128:(j + 1) * 128],
                                QIb[:, 512 + j * 128:512 + (j + 1) * 128], IDB[:]),
                                after=[tyg, tID] + (pPT.bw() if j == 0 else []))
                            R["QI"].read(tr)
                            trs.append(tr)
                        pPT.wrote(trs[-1])
                        dsl = Av1[:, og * 4:(og + 1) * 4, ttk * 128:ttk * 128 + 128]
                        tcp = vec(lambda e, o=dsl: e.tensor_copy(
                            o, PT[:].rearrange("p (j t) -> p j t", j=4)),
                            after=trs + (rA0.bw()
                                         if (tg4 == 0 and og == 0 and q == 0) else []))
                        pPT.read(tcp)
                        rA0.wrote(tcp)
                        g_last = tcp

            rRES = [Rgn(), Rgn()]
            res_last = None
            first_res = True
            for tg4 in range(2):
                for og in range(4):
                    mm_last = [None] * 4
                    for k in range(16):
                        wsrc = w_o[k * 128:(k + 1) * 128, og * 512:(og + 1) * 512]
                        twf = dma(WF[:, 0:512], wsrc, after=R["WF"].bw())
                        R["WF"].wrote(twf)
                        twb = sca(lambda e: e.activation(out=WB[:, 0:512],
                                                         in_=WF[:, 0:512], func=AF.Copy),
                                  after=[twf] + R["WB"].bw())
                        R["WF"].read(twb)
                        R["WB"].wrote(twb)
                        for q in range(4):
                            ttk = tg4 * 4 + q
                            lhs = Av1[:, k, ttk * 128:(ttk + 1) * 128]
                            deps = [twb] + (([g_last] + PMr[q].bw()) if k == 0 else [])
                            mm_last[q] = mm(PM[q][:], lhs, WB[:, 0:512],
                                            start=(k == 0), stop=(k == 15), after=deps)
                            rA0.read(mm_last[q])
                            if k == 0:
                                PMr[q].wrote(mm_last[q])
                        R["WB"].read(mm_last[3])
                    for q in range(4):
                        ttk = tg4 * 4 + q
                        PMr[q].wrote(mm_last[q])
                        slot = q % 2
                        rsl_c = C[:, 4096 + slot * 512:4608 + slot * 512]
                        rR = rRES[slot]
                        gd = (rTST.bw() + rSP0.bw()) if first_res else []
                        first_res = False
                        trl = dma(rsl_c, res_hbm.ap()[ttk * 128:(ttk + 1) * 128,
                                                      og * 512:(og + 1) * 512],
                                  after=R["resH"].br() + rR.bw() + gd)
                        rR.wrote(trl)
                        tadd = vec(lambda e, o=rsl_c, pm=PM[q]: e.tensor_tensor(
                            out=o, in0=o, in1=pm[:], op=ALU.add),
                            after=[trl, mm_last[q]])
                        PMr[q].read(tadd)
                        rR.wrote(tadd)
                        tst_ = dma(res_hbm.ap()[ttk * 128:(ttk + 1) * 128,
                                                og * 512:(og + 1) * 512], rsl_c,
                                   after=[tadd], eng="gpsimd")
                        rR.read(tst_)
                        res_last = tst_
                        if DEBUG & 4:
                            tdb = dma(dbg[4][ttk * 128:(ttk + 1) * 128,
                                             og * 512:(og + 1) * 512], rsl_c,
                                      after=[tadd], eng="gpsimd")
                            rR.read(tdb)
                            res_last = tdb
            for nm in ("A", "Bu", "C0", "C1", "C2", "Dp0", "Dp1", "QI", "resH",
                       "PT", "PSC", "PSY", "WF", "WB", "WDT", "T32"):
                R[nm].wrote(res_last)
            for i in range(4):
                PMr[i].wrote(res_last)

        tNW = None
        if PHASES & 2:
            emit_ffn(0)
        if PHASES & 4:
            emit_rwkv()
        if PHASES & 8:
            tNW = dma(WF[:, 0:2048], bass.AP(nw_row.tensor, 0, [[0, 128], [1, 2048]]),
                      after=R["WF"].bw())
            R["WF"].wrote(tNW)
            emit_ffn(1)

        final_toks = []
        for tt in range(8):
            sl, rsl = C[:, 2048:4096], R["C1"]
            tro = dma(sl, res_hbm.ap()[tt * 128:(tt + 1) * 128, :],
                      after=R["resH"].br() + rsl.bw())
            rsl.wrote(tro)
            tso = dma(out[tt * 128:(tt + 1) * 128, :], sl, after=[tro], eng="sync")
            rsl.read(tso)
            final_toks.append(tso)
        if DEBUG & 128:
            final_toks.append(dma(dbg[128], SM[:], after=final_toks[-1:], eng="gpsimd"))
            final_toks.append(dma(dbg[129], ag_gam_src.ap(), after=final_toks[-1:], eng="gpsimd"))
            final_toks.append(dma(dbg[130], ag_gam_dst.ap(), after=final_toks[-1:], eng="gpsimd"))
        if DEBUG & 256:
            final_toks.append(dma(dbg[256], Bu[:], after=final_toks[-1:] + [R["Bu"].w], eng="gpsimd"))
        if DEBUG & 512:
            final_toks.append(dma(dbg[512], A[:], after=final_toks[-1:] + [R["A"].w], eng="gpsimd"))
        E.op("sync", lambda e: e.nop(), after=final_toks)

        make = E.finalize(sems)
        block.sync(make("sync"))
        block.vector(make("vector"))
        block.scalar(make("scalar"))
        block.tensor(make("tensor"))
        block.gpsimd(make("gpsimd"))

    return nc


def _ct(x):
    return np.ascontiguousarray(x)


def make_in_maps(inp):
    f32 = np.float32
    x = np.asarray(inp["L0_attn_hidden_states"], f32)
    ipw = np.asarray(inp["L0_attn_in_proj_w"], f32)
    cw = np.asarray(inp["L0_attn_conv_w"], f32)
    cb = np.asarray(inp["L0_attn_conv_b"], f32)
    xpw = np.asarray(inp["L0_attn_x_proj_weight"], f32)
    dtw = np.asarray(inp["L0_attn_dt_proj_weight"], f32)
    dtb = np.asarray(inp["L0_attn_dt_proj_bias"], f32)
    alog = np.asarray(inp["L0_attn_A_log"], f32)
    dvec = np.asarray(inp["L0_attn_D"], f32)
    opw = np.asarray(inp["L0_attn_out_proj_w"], f32)
    gw = np.asarray(inp["L0_ffn_gate_w"], f32)
    gb = np.asarray(inp["L0_ffn_gate_b"], f32)
    vw = np.asarray(inp["L0_ffn_val_w"], f32)
    vb = np.asarray(inp["L0_ffn_val_b"], f32)
    ow = np.asarray(inp["L0_ffn_out_w"], f32)
    ob = np.asarray(inp["L0_ffn_out_b"], f32)
    maa = {k: np.asarray(inp["L1_attn_time_maa_" + k], f32) for k in
           ("x", "w", "k", "v", "r", "g")}
    maa1 = np.asarray(inp["L1_attn_time_maa_w1"], f32)
    maa2 = np.asarray(inp["L1_attn_time_maa_w2"], f32)
    tdec = np.asarray(inp["L1_attn_time_decay"], f32)
    dec1 = np.asarray(inp["L1_attn_time_decay_w1"], f32)
    dec2 = np.asarray(inp["L1_attn_time_decay_w2"], f32)
    faa = np.asarray(inp["L1_attn_time_faaaa"], f32)
    tst = np.asarray(inp["L1_attn_time_state"], f32)
    wr = np.asarray(inp["L1_attn_W_r"], f32)
    wk = np.asarray(inp["L1_attn_W_k"], f32)
    wv = np.asarray(inp["L1_attn_W_v"], f32)
    wg = np.asarray(inp["L1_attn_W_g"], f32)
    wo = np.asarray(inp["L1_attn_W_o"], f32)
    lnw = np.asarray(inp["L1_attn_ln_w"], f32)
    lnb = np.asarray(inp["L1_attn_ln_b"], f32)
    nw = np.asarray(inp["L1_ffn_norm_w"], f32)
    wgv = np.asarray(inp["L1_ffn_w_gv"], f32)
    wout1 = np.asarray(inp["L1_ffn_w_out"], f32)

    maa_stack = np.stack([maa["x"], maa["w"], maa["k"], maa["v"], maa["r"], maa["g"],
                          tdec, np.zeros(D, f32)], axis=1)
    identb = np.eye(128, dtype=BFD)
    identf = np.eye(128, dtype=f32)
    maskm = (np.arange(128)[:, None] < np.arange(128)[None, :]).astype(np.int32)

    in_maps = []
    for c in range(8):
        b, tau = c // 2, c % 2
        h = tau
        hsl = slice(HH * tau, HH * (tau + 1))
        m = {
            "xb": _ct(x[b]),
            "xsl": _ct(x[b, tau * TL:(tau + 1) * TL]),
            "w_inx": _ct(ipw[2 * D * h:2 * D * (h + 1):2, :].T),
            "w_ing": _ct(ipw[2 * D * h + 1:2 * D * (h + 1):2, :].T),
            "convw": _ct(cw[D * h:D * (h + 1)]),
            "convb": _ct(cb[D * h:D * (h + 1)].reshape(16, 128).T),
            "w_xp": _ct(xpw[h].T),
            "w_dt": _ct(dtw[h].T),
            "dt_bias": _ct(dtb[h].reshape(16, 128).T),
            "a_log": _ct(alog[h]),
            "d_vec": _ct(dvec[h].reshape(16, 128).T),
            "w_op": _ct(opw[:, D * h:D * (h + 1)].T),
            "w_gate": _ct(gw[hsl].T),
            "w_val": _ct(vw[hsl].T),
            "w_out0": _ct(ow[:, hsl].T),
            "b_gate": _ct(gb[hsl][None, :]),
            "b_val": _ct(vb[hsl][None, :]),
            "b_out0": _ct(ob[None, :]),
            "maa_c": _ct(maa_stack.T.reshape(8 * 16, 128).T),
            "w_maa1": _ct(maa1),
            "w_maa2": _ct(maa2.reshape(5 * 32, D)),
            "w_dec1": _ct(dec1),
            "w_dec2": _ct(dec2),
            "u_faa": _ct(faa.reshape(-1).reshape(16, 128).T),
            "t_state": _ct(tst.reshape(D, N1)),
            "w_r": _ct(wr.T), "w_k": _ct(wk.T), "w_v": _ct(wv.T),
            "w_g": _ct(wg.T), "w_o": _ct(wo.T),
            "lnw_row": _ct(lnw[None, :]), "lnb_row": _ct(lnb[None, :]),
            "w_gv_v": _ct(wgv[HH * tau:HH * (tau + 1)].T),
            "w_gv_g": _ct(wgv[HID + HH * tau:HID + HH * (tau + 1)].T),
            "w_out1": _ct(wout1[:, hsl].T),
            "nw_row": _ct(nw[None, :]),
            "cflag": np.array([[float(tau), float(1 - tau)] + [0.0] * 6], dtype=f32),
            "identb": identb,
            "maskst": maskm,
            "identf": identf,
        }
        in_maps.append(m)
    return in_maps


_nc_cache = {}


def kernel(**inputs):
    key = (DEBUG, PHASES)
    if key not in _nc_cache:
        _nc_cache[key] = build_program()
    nc = _nc_cache[key]
    in_maps = make_in_maps(inputs)
    res = run_bass_kernel_spmd(nc, in_maps, core_ids=list(range(8)))
    outp = np.zeros((B, S, D), np.float32)
    for c in range(8):
        b, tau = c // 2, c % 2
        outp[b, tau * TL:(tau + 1) * TL] = res.results[c]["out"]
    if DEBUG:
        kernel._dbg = res.results
    return outp
